# version 11
# baseline (speedup 1.0000x reference)
import numpy as np
import concourse.bass as bass
import concourse.mybir as mybir
from concourse.bass_utils import run_bass_kernel_spmd
from contextlib import ExitStack

F32 = mybir.dt.float32
BF16 = mybir.dt.bfloat16
AF = mybir.ActivationFunctionType
ALU = mybir.AluOpType
AX = mybir.AxisListType

D = 1024
NDC = 8
TCORE = 8192
CHUNK = 1024
TG = 512
NTG = CHUNK // TG
NTPT = TG // 128
E = 20
NSLOT = 4
H4 = 256
EPS = 1e-5


class Ctr:
    def __init__(self, h):
        self.h = h
        self.n = 0

    def inc(self, instr, k=1):
        instr.then_inc(self.h, k)
        self.n += k
        return self.n


class Waiter:
    def __init__(self):
        self.seen = {}

    def wait(self, eng, ctr, val):
        if val <= 0:
            return
        key = id(ctr)
        if self.seen.get(key, -1) >= val:
            return
        eng.wait_ge(ctr.h, val)
        self.seen[key] = val


class Bank:
    def __init__(self, t):
        self.t = t
        self.pe_done = 0
        self.released = None


def build(n_chunk):
    nc = bass.Bass("TRN2", target_bir_lowering=False)
    dt = nc.dram_tensor
    x_fm = dt("x_fm", [D, TCORE], F32, kind="ExternalInput").ap()
    wkt = dt("wkt", [D, D], F32, kind="ExternalInput").ap()
    w1t = dt("w1t", [D, 64], F32, kind="ExternalInput").ap()
    w2t = dt("w2t", [64, D], F32, kind="ExternalInput").ap()
    wg1 = dt("wg1", [D, H4], F32, kind="ExternalInput").ap()
    wg2 = dt("wg2", [H4, 40], F32, kind="ExternalInput").ap()
    bg1m = dt("bg1m", [128, 2], F32, kind="ExternalInput").ap()
    bg2r = dt("bg2r", [1, E], F32, kind="ExternalInput").ap()
    be = dt("be", [E, D], F32, kind="ExternalInput").ap()
    we = dt("we", [E * D, D], F32, kind="ExternalInput").ap()
    jm = dt("jm", [128, 128], F32, kind="ExternalInput").ap()
    onescol = dt("onescol", [128, 1], F32, kind="ExternalInput").ap()
    ones1 = dt("ones1", [1, 128], F32, kind="ExternalInput").ap()
    idn = dt("idn", [128, 128], F32, kind="ExternalInput").ap()
    sel = dt("sel", [E, E * 128], F32, kind="ExternalInput").ap()
    epsc = dt("epsc", [1, 1], F32, kind="ExternalInput").ap()
    out_fm = dt("out_fm", [D, TCORE], F32, kind="ExternalOutput").ap()

    we_r = we.rearrange("(e dc p) o -> e p dc o", e=E, dc=NDC, p=128)
    x_r = x_fm.rearrange("(dc p) t -> p dc t", p=128)
    o_r = out_fm.rearrange("(dc p) t -> p dc t", p=128)

    with ExitStack() as es:
        sb = lambda name, shape, dtype: es.enter_context(nc.sbuf_tensor(name, shape, dtype))
        ps = lambda name, shape, dtype: es.enter_context(nc.psum_tensor(name, shape, dtype))

        X = sb("X", [128, NDC * CHUNK], F32)
        ZN = sb("ZN", [128, NDC * CHUNK], BF16)
        SQ = sb("SQ", [128, 2 * NDC * TG], BF16)
        WS = sb("WS", [128, NSLOT * NDC * D], BF16)
        ZE = sb("ZE", [128, 4 * NDC * TG], BF16)
        ROWA = sb("ROWA", [1, 2 * TG], F32)
        ROWB = sb("ROWB", [1, 2 * TG], F32)
        HB = sb("HB", [128, 2 * TG], BF16)
        ABUF = sb("ABUF", [64, TG], BF16)
        GBB = sb("GBB", [128, 3 * CHUNK], BF16)
        MBS = sb("MBS", [128, 2 * TG], BF16)
        RBS = sb("RBS", [128, 2 * TG], BF16)
        GTB = sb("GTB", [E, CHUNK], BF16)
        GBT = sb("GBT", [128, NTG * NTPT * E], BF16)
        PRB = sb("PRB", [128, NTPT * E], F32)
        MSK = sb("MSK", [128, NTPT * E], F32)
        TM2 = sb("TM2", [128, NTPT * E], F32)
        NRM = sb("NRM", [128, NTPT], F32)
        SUMS = sb("SUMS", [128, NTPT], F32)
        RS = sb("RS", [128, NTPT], F32)
        M1 = sb("M1", [128, NTPT], F32)
        W1TB = sb("W1TB", [128, NDC * 64], BF16)
        W2TB = sb("W2TB", [64, D], BF16)
        WG1B = sb("WG1B", [128, NDC * H4], BF16)
        WG2B = sb("WG2B", [128, 2 * E], BF16)
        BEB = sb("BEB", [E, D], BF16)
        BG1C = sb("BG1C", [128, 2], F32)
        BG2R = sb("BG2R", [1, E], BF16)
        JB = sb("JB", [128, 128], BF16)
        OCB = sb("OCB", [128, 1], BF16)
        O1B = sb("O1B", [1, 128], BF16)
        IDNB = sb("IDNB", [128, 128], BF16)
        SELB = sb("SELB", [E, E * 128], BF16)
        EPSB = sb("EPSB", [1, 1], F32)

        banks = [Bank(ps(f"pb{i}", [128, 512], F32)) for i in range(7)]
        pGT = ps("pGT", [E, 128], BF16)
        pgt_pe_done = [0]
        pgt_released = [None]

        sT = Ctr(es.enter_context(nc.semaphore("sT")))
        sV = Ctr(es.enter_context(nc.semaphore("sV")))
        sS = Ctr(es.enter_context(nc.semaphore("sS")))
        sG = Ctr(es.enter_context(nc.semaphore("sG")))
        sW = Ctr(es.enter_context(nc.semaphore("sW")))
        sWS = [Ctr(es.enter_context(nc.semaphore(f"sWS{i}"))) for i in range(NSLOT)]
        sXd = [Ctr(es.enter_context(nc.semaphore(f"sXd{i}"))) for i in range(NDC)]
        sOd = [Ctr(es.enter_context(nc.semaphore(f"sOd{i}"))) for i in range(NDC)]

        blk = es.enter_context(nc.Block())
        T, V, S, G, SY = nc.tensor, nc.vector, nc.scalar, nc.gpsimd, nc.sync
        wt, wv, ws, wg, wy = Waiter(), Waiter(), Waiter(), Waiter(), Waiter()

        def pe_write(bank):
            if bank.released is not None:
                wt.wait(T, bank.released[0], bank.released[1])

        def pe_done(bank, instr):
            bank.pe_done = sT.inc(instr)

        def reader_wait(w, eng, bank):
            w.wait(eng, sT, bank.pe_done)

        def release(bank, ctr, instr, pre=None):
            n = pre if pre is not None else ctr.inc(instr)
            bank.released = (ctr, n)
            return n

        G.dma_start(W1TB[:].rearrange("p (kc d) -> p kc d", kc=NDC),
                    w1t.rearrange("(kc p) d -> p kc d", p=128)).then_inc(sW.h, 16)
        G.dma_start(W2TB[:], w2t).then_inc(sW.h, 16)
        G.dma_start(WG1B[:].rearrange("p (dc h) -> p dc h", dc=NDC),
                    wg1.rearrange("(dc p) h -> p dc h", p=128)).then_inc(sW.h, 16)
        G.dma_start(WG2B[:].rearrange("p (hc e) -> p hc e", hc=2),
                    wg2.rearrange("(hc p) e -> p hc e", p=128)[:, :, 0:E]).then_inc(sW.h, 16)
        G.dma_start(BEB[:], be).then_inc(sW.h, 16)
        G.dma_start(BG2R[:], bg2r).then_inc(sW.h, 16)
        G.dma_start(JB[:], jm).then_inc(sW.h, 16)
        G.dma_start(OCB[:], onescol).then_inc(sW.h, 16)
        G.dma_start(O1B[:], ones1).then_inc(sW.h, 16)
        G.dma_start(IDNB[:], idn).then_inc(sW.h, 16)
        G.dma_start(SELB[:], sel).then_inc(sW.h, 16)
        G.dma_start(BG1C[:], bg1m).then_inc(sW.h, 16)
        G.dma_start(EPSB[:], epsc).then_inc(sW.h, 16)
        sW.n += 16 * 13
        init_w = sW.n

        slots_free = [0] * NSLOT
        slot_loaded = [0] * NSLOT
        slot_rr = [0]

        def issue_weight_load(kind, e=None):
            s = slot_rr[0] % NSLOT
            slot_rr[0] += 1
            wg.wait(G, sT, slots_free[s])
            dst = WS[:, s * NDC * D:(s + 1) * NDC * D].rearrange("p (dc o) -> p dc o", dc=NDC)
            if kind == "we":
                G.dma_start(dst, we_r[e]).then_inc(sWS[s].h, 16)
            else:
                G.dma_start(dst, wkt.rearrange("(hc p) k -> p hc k", p=128)).then_inc(sWS[s].h, 16)
            sWS[s].n += 16
            slot_loaded[s] = sWS[s].n
            return s

        ln_pool_prev = [0, 0]
        rbs_guard = [[], []]

        class LNPhase:

            def __init__(self, x_dep):
                self.x_dep = x_dep
                self.tgb = {0: (banks[2], banks[3]), 1: (banks[4], banks[5])}
                self.cast_a = {}
                self.cast_v = {}
                self.subs_done = {}
                self.sq_done = {}
                self.rstd_done = {}
                self.zn_ready = [0, 0]
                self.done_cast = set()
                self.done = set()

            def cast(self, tg, dc):
                if (tg, dc) in self.done_cast:
                    return
                self.done_cast.add((tg, dc))
                t0 = tg * TG
                sl = slice((tg * NDC + dc) * TG, (tg * NDC + dc + 1) * TG)
                xs = X[:, dc * CHUNK + t0:dc * CHUNK + t0 + TG]
                if dc % 2 == 0:
                    for (ctr, cnt) in self.x_dep(tg, dc):
                        ws.wait(S, ctr, cnt)
                    i = S.activation(SQ[:, sl], xs, AF.Copy)
                    self.cast_a[tg] = sS.inc(i)
                else:
                    for (ctr, cnt) in self.x_dep(tg, dc):
                        wv.wait(V, ctr, cnt)
                    i = V.tensor_copy(SQ[:, sl], xs)
                    if dc == NDC - 1:
                        i = V.drain()
                    self.cast_v[tg] = sV.inc(i)

            def casts(self, tg):
                for dc in range(NDC):
                    self.cast(tg, dc)

            def stats_mean(self, tg):
                if ("stats_mean", tg) in self.done:
                    return
                self.done.add(("stats_mean", tg))
                self.casts(tg)
                bM, bR = self.tgb[tg]
                pe_write(bM)
                wt.wait(T, sS, self.cast_a[tg])
                wt.wait(T, sV, self.cast_v[tg])
                for dc in range(NDC):
                    i = T.matmul(bM.t[:], JB[:],
                                 SQ[:, (tg * NDC + dc) * TG:(tg * NDC + dc + 1) * TG],
                                 start=(dc == 0), stop=(dc == NDC - 1))
                pe_done(bM, i)
                reader_wait(ws, S, bM)
                ws.wait(S, sG, ln_pool_prev[tg])
                for (ctr, cnt) in rbs_guard[tg]:
                    ws.wait(S, ctr, cnt)
                i = S.activation(MBS[:, tg * TG:(tg + 1) * TG], bM.t[:], AF.Copy)
                mbs_done = sS.inc(i)
                bM.released = (sS, mbs_done)
                t0 = tg * TG
                rsl = slice(tg * TG, (tg + 1) * TG)
                wv.wait(V, sS, mbs_done)
                for dc in range(NDC):
                    sl = slice((tg * NDC + dc) * TG, (tg * NDC + dc + 1) * TG)
                    i = V.tensor_tensor(ZN[:, dc * CHUNK + t0:dc * CHUNK + t0 + TG],
                                        SQ[:, sl], MBS[:, rsl], ALU.subtract)
                i = V.drain()
                self.subs_done[tg] = sV.inc(i)

            def stats_s2(self, tg):
                if ("stats_s2", tg) in self.done:
                    return
                self.done.add(("stats_s2", tg))
                self.stats_mean(tg)
                bR = self.tgb[tg][1]
                for dc in range(NDC):
                    sl = slice((tg * NDC + dc) * TG, (tg * NDC + dc + 1) * TG)
                    i = V.tensor_tensor(SQ[:, sl], SQ[:, sl], SQ[:, sl], ALU.mult)
                i = V.drain()
                self.sq_done[tg] = sV.inc(i)
                pe_write(bR)
                wt.wait(T, sV, self.sq_done[tg])
                for dc in range(NDC):
                    i = T.matmul(bR.t[0:1, :], OCB[:],
                                 SQ[:, (tg * NDC + dc) * TG:(tg * NDC + dc + 1) * TG],
                                 start=(dc == 0), stop=(dc == NDC - 1))
                pe_done(bR, i)

            def rows(self, tg):
                if ("rows", tg) in self.done:
                    return
                self.done.add(("rows", tg))
                self.stats_s2(tg)
                bR = self.tgb[tg][1]
                rsl = slice(tg * TG, (tg + 1) * TG)
                reader_wait(ws, S, bR)
                S.drain()
                S.activation(ROWA[0:1, rsl], bR.t[0:1, :], AF.Copy, scale=1.0 / D)
                i = S.activation(ROWB[0:1, rsl], MBS[0:1, rsl], AF.Square)
                rows_done = sS.inc(i)
                wv.wait(V, sS, rows_done)
                V.tensor_tensor(ROWA[0:1, rsl], ROWA[0:1, rsl], ROWB[0:1, rsl], ALU.subtract)
                i = V.drain()
                var_done = sV.inc(i)
                ws.wait(S, sV, var_done)
                S.activation(ROWB[0:1, rsl], ROWA[0:1, rsl], AF.Ln, bias=EPSB[0:1, :])
                S.drain()
                i = S.activation(RBS[0:1, rsl], ROWB[0:1, rsl], AF.Exp, scale=-0.5)
                self.rstd_done[tg] = sS.inc(i)

            def bcast(self, tg):
                if ("bcast", tg) in self.done:
                    return
                self.done.add(("bcast", tg))
                self.rows(tg)
                bR = self.tgb[tg][1]
                rsl = slice(tg * TG, (tg + 1) * TG)
                wt.wait(T, sS, self.rstd_done[tg])
                i = T.matmul(bR.t[:], O1B[:], RBS[0:1, rsl], start=True, stop=True)
                pe_done(bR, i)
                reader_wait(ws, S, bR)
                i = S.activation(RBS[:, rsl], bR.t[:], AF.Copy)
                self.rbs_done = sS.inc(i)
                release(bR, sS, i, pre=self.rbs_done)

            def apply(self, tg):
                if ("apply", tg) in self.done:
                    return
                self.done.add(("apply", tg))
                self.bcast(tg)
                rsl = slice(tg * TG, (tg + 1) * TG)
                t0 = tg * TG
                wv.wait(V, sS, self.rbs_done)
                for dc in range(NDC):
                    i = V.tensor_tensor(ZN[:, dc * CHUNK + t0:dc * CHUNK + t0 + TG],
                                        ZN[:, dc * CHUNK + t0:dc * CHUNK + t0 + TG],
                                        RBS[:, rsl], ALU.mult)
                i = V.drain()
                dve_mul_done = sV.inc(i)
                wg.wait(G, sV, dve_mul_done)
                i = G.engine_nop()
                self.zn_ready[tg] = sG.inc(i)
                ln_pool_prev[tg] = self.zn_ready[tg]
                rbs_guard[tg] = []

            def finish(self):
                for tg in range(NTG):
                    self.apply(tg)
                return self.zn_ready

        def emit_prep(zn_ready, hook=None, stage_hooks=None):
            chains = {}

            def tg_block(tg, shooks):
                t0 = tg * TG
                hb_banks = []
                for hc in range(2):
                    b = banks[hc]
                    pe_write(b)
                    wt.wait(T, sG, zn_ready[tg])
                    for dc in range(NDC):
                        i = T.matmul(b.t[:], WG1B[:, dc * H4 + hc * 128:dc * H4 + hc * 128 + 128],
                                     ZN[:, dc * CHUNK + t0:dc * CHUNK + t0 + TG],
                                     start=(dc == 0), stop=(dc == NDC - 1))
                    pe_done(b, i)
                    hb_banks.append(b)
                for hc in range(2):
                    b = hb_banks[hc]
                    reader_wait(ws, S, b)
                    i = S.activation(HB[:, hc * TG:(hc + 1) * TG], b.t[:], AF.Gelu,
                                     bias=BG1C[:, hc:hc + 1])
                    release(b, sS, i)
                if shooks and "h0" in shooks:
                    shooks["h0"]()
                hb_done = sS.n
                lb = banks[2 + tg]
                pe_write(lb)
                wt.wait(T, sS, hb_done)
                for ts in range(NTPT):
                    T.matmul(lb.t[:, ts * 32:ts * 32 + E],
                             HB[:, 0 * TG + ts * 128:0 * TG + ts * 128 + 128],
                             WG2B[:, 0:E], start=True, stop=False)
                    T.matmul(lb.t[:, ts * 32:ts * 32 + E],
                             HB[:, 1 * TG + ts * 128:1 * TG + ts * 128 + 128],
                             WG2B[:, E:2 * E], start=False, stop=False)
                    i = T.matmul(lb.t[:, ts * 32:ts * 32 + E], O1B[:], BG2R[:],
                                 start=False, stop=True)
                pe_done(lb, i)
                if shooks and "l0" in shooks:
                    shooks["l0"]()
                lb3 = bass.AP(lb.t[:].tensor, 0, [[512, 128], [32, NTPT], [1, E]])
                reader_wait(wv, V, lb)
                V.tensor_reduce(NRM[:], lb3, axis=AX.X, op=ALU.max, negate=True)
                i = V.drain()
                nrm_done = sV.inc(i)
                for ts in range(NTPT):
                    ws.wait(S, sV, nrm_done)
                    i = S.activation(PRB[:, ts * E:(ts + 1) * E], lb.t[:, ts * 32:ts * 32 + E],
                                     AF.Exp, bias=NRM[:, ts:ts + 1], accum_out=SUMS[:, ts:ts + 1])
                ed = sS.inc(i)
                release(lb, sS, i, pre=ed)
                prb3 = PRB[:].rearrange("p (s e) -> p s e", e=E)
                msk3 = MSK[:].rearrange("p (s e) -> p s e", e=E)
                tm23 = TM2[:].rearrange("p (s e) -> p s e", e=E)
                m2b = bass.AP(M1[:].tensor, 0, [[NTPT, 128], [1, NTPT], [0, E]])
                rsb = bass.AP(RS[:].tensor, 0, [[NTPT, 128], [1, NTPT], [0, E]])
                wv.wait(V, sS, ed)
                V.tensor_scalar(MSK[:], PRB[:], 1.0, None, ALU.is_ge)
                V.reciprocal(RS[:], SUMS[:])
                V.drain()
                V.scalar_tensor_tensor(MSK[:], MSK[:], -1e9, PRB[:], ALU.mult, ALU.add)
                V.drain()
                V.tensor_reduce(M1[:], msk3, axis=AX.X, op=ALU.max)
                V.drain()
                V.tensor_tensor(msk3, prb3, m2b, ALU.is_ge)
                V.tensor_tensor(tm23, prb3, rsb, ALU.mult)
                V.drain()
                V.tensor_tensor(GBT[:, tg * NTPT * E:(tg + 1) * NTPT * E], TM2[:], MSK[:],
                                ALU.mult)
                i = V.drain()
                chains[tg] = sV.inc(i)

            def transposes(tg):
                for ts in range(NTPT):
                    if pgt_released[0] is not None:
                        wt.wait(T, pgt_released[0][0], pgt_released[0][1])
                    wt.wait(T, sV, chains[tg])
                    i = T.transpose(pGT[:], GBT[:, (tg * NTPT + ts) * E:(tg * NTPT + ts + 1) * E],
                                    IDNB[:])
                    pgt_pe_done[0] = sT.inc(i)
                    wv.wait(V, sT, pgt_pe_done[0])
                    i = V.tensor_copy(GTB[:, (tg * NTPT + ts) * 128:(tg * NTPT + ts) * 128 + 128],
                                      pGT[:])
                    sV.inc(i)
                    pgt_released[0] = (sV, sV.n)
                i = V.drain()
                sV.inc(i)
                if hook is not None:
                    hook(tg)

            tg_block(0, stage_hooks)
            if stage_hooks and "c0" in stage_hooks:
                stage_hooks["c0"]()
            transposes(0)
            if stage_hooks and "t0" in stage_hooks:
                stage_hooks["t0"]()
            tg_block(1, None)
            transposes(1)

        z_free = [0] * 4
        gbb_free = {}

        class ExpertPhase:

            def __init__(self):
                self.gb_banks = [banks[2], banks[3], banks[4], banks[5]]
                self.zorder = []
                for pair in range(E // 2):
                    for tg in range(NTG):
                        for e in (2 * pair, 2 * pair + 1):
                            self.zorder.append((e, tg))
                self.z_done = {}
                self.z_sl = {}
                self.gbb_req = {}
                self.zq = 0
                self.gb_emitted = set()

            def gb_half(self, pair, tg):
                for j in range(2):
                    e = 2 * pair + j
                    b = self.gb_banks[2 * j + tg]
                    pe_write(b)
                    wt.wait(T, sV, self.gtb_done[tg])
                    i = T.matmul(b.t[:], SELB[:, e * 128:(e + 1) * 128],
                                 GTB[:, tg * TG:(tg + 1) * TG], start=True, stop=True)
                    pe_done(b, i)
                for j in range(2):
                    e = 2 * pair + j
                    zs = e % 3
                    b = self.gb_banks[2 * j + tg]
                    ws.wait(S, sV, gbb_free.get((zs, tg), 0))
                    reader_wait(ws, S, b)
                    i = S.activation(GBB[:, zs * CHUNK + tg * TG:zs * CHUNK + (tg + 1) * TG],
                                     b.t[:], AF.Copy)
                    release(b, sS, i)
                    self.gbb_req[(e, tg)] = sS.n
                self.gb_emitted.add((pair, tg))

            def z_pump(self, pe_reachable):
                while self.zq < len(self.zorder):
                    e, tg = self.zorder[self.zq]
                    if (e, tg) not in self.gbb_req:
                        break
                    slot = self.zq % 4
                    if z_free[slot] is None or z_free[slot] > pe_reachable:
                        break
                    prev_free = z_free[slot]
                    self.zq += 1
                    self.z_sl[(e, tg)] = slot
                    z_free[slot] = None
                    wv.wait(V, sS, self.gbb_req[(e, tg)])
                    wv.wait(V, sT, prev_free)
                    wv.wait(V, sG, self.zn_ready[tg])
                    zs = e % 3
                    for dc in range(NDC):
                        V.tensor_tensor(
                            ZE[:, slot * NDC * TG + dc * TG:slot * NDC * TG + (dc + 1) * TG],
                            ZN[:, dc * CHUNK + tg * TG:dc * CHUNK + tg * TG + TG],
                            GBB[:, zs * CHUNK + tg * TG:zs * CHUNK + (tg + 1) * TG],
                            ALU.mult)
                    i = V.drain()
                    n = sV.inc(i)
                    self.z_done[(e, tg)] = n
                    gbb_free[(zs, tg)] = n

            def prep_hook(self, tg):
                self.gb_half(0, tg)
                self.z_pump(sT.n)

            def run(self, eslots, post_pair=None, tail_hook=None, last_evac=None):
                if last_evac is None:
                    last_evac = {}
                for pair in range(E // 2):
                    e0, e1 = 2 * pair, 2 * pair + 1
                    s0, s1 = eslots[e0], eslots[e1]
                    if pair + 1 < E // 2:
                        for tg in range(NTG):
                            self.gb_half(pair + 1, tg)
                    wt.wait(T, sWS[s0], slot_loaded[s0])
                    wt.wait(T, sWS[s1], slot_loaded[s1])
                    for tg in range(NTG):
                        wt.wait(T, sV, self.z_done[(e0, tg)])
                        wt.wait(T, sV, self.z_done[(e1, tg)])
                        zq0, zq1 = self.z_sl[(e0, tg)], self.z_sl[(e1, tg)]
                        for oc in range(NDC):
                            b = banks[[0, 1, 6][oc % 3]]
                            pe_write(b)
                            first = True
                            if pair == 0:
                                T.matmul(b.t[:], BEB[:, oc * 128:(oc + 1) * 128],
                                         GTB[:, tg * TG:(tg + 1) * TG], start=True, stop=False)
                                first = False
                            for j, (e, s, zz) in enumerate(((e0, s0, zq0), (e1, s1, zq1))):
                                for dc in range(NDC):
                                    i = T.matmul(
                                        b.t[:],
                                        WS[:, s * NDC * D + dc * D + oc * 128:
                                              s * NDC * D + dc * D + oc * 128 + 128],
                                        ZE[:, zz * NDC * TG + dc * TG:zz * NDC * TG + dc * TG + TG],
                                        start=first, stop=(j == 1 and dc == NDC - 1))
                                    first = False
                                    if oc == NDC - 1 and dc == NDC - 1 and j == 0:
                                        z_free[zq0] = sT.inc(i)
                                        if tg == NTG - 1:
                                            slots_free[s0] = z_free[zq0]
                            pe_done(b, i)
                            if oc == NDC - 1:
                                z_free[zq1] = sT.n
                                if tg == NTG - 1:
                                    slots_free[s1] = sT.n
                            stop_n = sT.n
                            wv.wait(V, sT, stop_n)
                            dst = X[:, oc * CHUNK + tg * TG:oc * CHUNK + tg * TG + TG]
                            i = V.scalar_tensor_tensor(dst, b.t[:], 0.5, dst,
                                                       ALU.mult, ALU.add)
                            release(b, sV, i)
                            if pair == E // 2 - 1:
                                i = V.drain()
                                last_evac[(tg, oc)] = sV.inc(i)
                                if tail_hook is not None:
                                    tail_hook(tg, oc)
                            self.z_pump(stop_n)
                    if post_pair is not None:
                        post_pair(pair)
                i = V.drain()
                return sV.inc(i), last_evac

        def emit_attn(wkt_slot, zn_ready, hook=None, att_evac=None, k_hooks=None):
            if att_evac is None:
                att_evac = {}
            k_hooks = k_hooks or {}
            for tg in range(NTG):
                t0 = tg * TG
                kb_dones = []
                for kc in range(NDC):
                    b = banks[kc % 2]
                    pe_write(b)
                    wt.wait(T, sG, zn_ready[tg])
                    wt.wait(T, sWS[wkt_slot], slot_loaded[wkt_slot])
                    for hc in range(NDC):
                        i = T.matmul(b.t[:],
                                     WS[:, wkt_slot * NDC * D + hc * D + kc * 128:
                                           wkt_slot * NDC * D + hc * D + kc * 128 + 128],
                                     ZN[:, hc * CHUNK + t0:hc * CHUNK + t0 + TG],
                                     start=(hc == 0), stop=(hc == NDC - 1))
                    pe_done(b, i)
                    reader_wait(ws, S, b)
                    i = S.activation(SQ[:, (tg * NDC + kc) * TG:(tg * NDC + kc + 1) * TG],
                                     b.t[:], AF.Copy)
                    release(b, sS, i)
                    kb_dones.append(sS.n)
                    if tg == 0 and kc in k_hooks:
                        k_hooks[kc]()
                b = banks[4]
                pe_write(b)
                for kc in range(NDC):
                    wt.wait(T, sS, kb_dones[kc])
                    i = T.matmul(b.t[0:64, :], W1TB[:, kc * 64:(kc + 1) * 64],
                                 SQ[:, (tg * NDC + kc) * TG:(tg * NDC + kc + 1) * TG],
                                 start=(kc == 0), stop=(kc == NDC - 1))
                pe_done(b, i)
                reader_wait(ws, S, b)
                i = S.activation(ABUF[:], b.t[0:64, :], AF.Gelu)
                ab_done = sS.inc(i)
                release(b, sS, i, pre=ab_done)
                for hc in range(NDC):
                    b = banks[hc % 2]
                    pe_write(b)
                    wt.wait(T, sS, ab_done)
                    i = T.matmul(b.t[:], W2TB[:, hc * 128:(hc + 1) * 128], ABUF[:],
                                 start=True, stop=True)
                    pe_done(b, i)
                    reader_wait(wv, V, b)
                    i = V.tensor_tensor(X[:, hc * CHUNK + t0:hc * CHUNK + t0 + TG],
                                        X[:, hc * CHUNK + t0:hc * CHUNK + t0 + TG],
                                        b.t[:], ALU.add)
                    release(b, sV, i)
                    i = V.drain()
                    att_evac[(tg, hc)] = sV.inc(i)
                    if hook is not None:
                        hook(tg, hc)
            return sV.n, att_evac

        pending_ln1 = [None]

        def mk_ln_hook(ln, extra=None):
            def hook(tg, idx):
                if extra is not None:
                    extra(tg, idx)
                ln.cast(tg, idx)
                if tg == 1:
                    if idx == 0:
                        ln.stats_mean(0)
                    elif idx == 4:
                        ln.stats_s2(0)
                    elif idx == 5:
                        ln.rows(0)
                    elif idx == 6:
                        ln.bcast(0)
                    elif idx == 7:
                        ln.apply(0)
            return hook

        for c in range(n_chunk):
            if pending_ln1[0] is None:
                x_here = []
                for dc in range(NDC):
                    SY.dma_start(X[:, dc * CHUNK:(dc + 1) * CHUNK],
                                 x_r[:, dc, c * CHUNK:(c + 1) * CHUNK]).then_inc(sXd[dc].h, 16)
                    sXd[dc].n += 16
                    x_here.append(sXd[dc].n)
                xh = list(x_here)
                ln1 = LNPhase(lambda tg, dc: [(sXd[dc], xh[dc])])
            else:
                ln1 = pending_ln1[0]
            if c == 0:
                ws.wait(S, sW, init_w)
                wt.wait(T, sW, init_w)
                wv.wait(V, sW, init_w)
                wg.wait(G, sW, init_w)
            ln1.apply(0)

            my1 = {e: issue_weight_load("we", e) for e in range(4)}
            ep1 = ExpertPhase()
            ep1.zn_ready = ln1.zn_ready
            ep1.gtb_done = [0, 0]
            ln1.stats_mean(1)
            emit_prep(ln1.zn_ready,
                      hook=lambda tg: (ep1.gtb_done.__setitem__(tg, sV.n),
                                       ep1.prep_hook(tg)),
                      stage_hooks={"h0": lambda: ln1.stats_s2(1),
                                   "l0": lambda: ln1.rows(1),
                                   "c0": lambda: ln1.bcast(1),
                                   "t0": lambda: ln1.apply(1)})
            f2pre = {}

            def post_pair1(p):
                for e in (4 + 2 * p, 5 + 2 * p):
                    if e < E:
                        my1[e] = issue_weight_load("we", e)
                if p == 8:
                    f2pre["wkt"] = issue_weight_load("wkt")
                if p == 9:
                    f2pre[0] = issue_weight_load("we", 0)
                    f2pre[1] = issue_weight_load("we", 1)

            f1_evac = {}
            ln2 = LNPhase(lambda tg, dc: [(sV, f1_evac[(tg, dc)])])
            ep1.run(my1, post_pair1, tail_hook=mk_ln_hook(ln2), last_evac=f1_evac)
            ln2.stats_mean(1)
            att_evac = {}
            ln3 = LNPhase(lambda tg, dc: [(sV, att_evac[(tg, dc)])])
            emit_attn(f2pre["wkt"], ln2.zn_ready, hook=mk_ln_hook(ln3), att_evac=att_evac,
                      k_hooks={1: lambda: ln2.stats_s2(1),
                               3: lambda: ln2.rows(1),
                               5: lambda: ln2.bcast(1),
                               6: lambda: ln2.apply(1)})
            slots_free[f2pre["wkt"]] = sT.n
            f2pre[2] = issue_weight_load("we", 2)
            f2pre[3] = issue_weight_load("we", 3)
            ln3.apply(0)
            ln3.stats_mean(1)
            ep2 = ExpertPhase()
            ep2.zn_ready = ln3.zn_ready
            ep2.gtb_done = [0, 0]
            emit_prep(ln3.zn_ready,
                      hook=lambda tg: (ep2.gtb_done.__setitem__(tg, sV.n),
                                       ep2.prep_hook(tg)),
                      stage_hooks={"h0": lambda: ln3.stats_s2(1),
                                   "l0": lambda: ln3.rows(1),
                                   "c0": lambda: ln3.bcast(1),
                                   "t0": lambda: ln3.apply(1)})
            my2 = {e: f2pre[e] for e in range(4)}

            def post_pair2(p):
                for e in (4 + 2 * p, 5 + 2 * p):
                    if e < E:
                        my2[e] = issue_weight_load("we", e)

            f2_evac = {}
            nxt_here = {}

            def store_load(tg, dc):
                hsl = slice(dc * CHUNK + tg * TG, dc * CHUNK + (tg + 1) * TG)
                wy.wait(SY, sV, f2_evac[(tg, dc)])
                wy.wait(SY, sOd[dc], sOd[dc].n)
                wy.wait(SY, sXd[dc], sXd[dc].n)
                SY.dma_start(o_r[:, dc, c * CHUNK + tg * TG:c * CHUNK + (tg + 1) * TG],
                             X[:, hsl]).then_inc(sOd[dc].h, 16)
                sOd[dc].n += 16
                if c + 1 < n_chunk:
                    wy.wait(SY, sOd[dc], sOd[dc].n)
                    SY.dma_start(X[:, hsl],
                                 x_r[:, dc, (c + 1) * CHUNK + tg * TG:
                                            (c + 1) * CHUNK + (tg + 1) * TG]).then_inc(sXd[dc].h, 16)
                    sXd[dc].n += 16
                    nxt_here[(tg, dc)] = sXd[dc].n

            if c + 1 < n_chunk:
                lnext = LNPhase(lambda tg, dc: [(sXd[dc], nxt_here[(tg, dc)])])
                hook2 = mk_ln_hook(lnext, extra=store_load)
                pending_ln1[0] = lnext
            else:
                def hook2(tg, dc):
                    store_load(tg, dc)
                pending_ln1[0] = None
            ep2.run(my2, post_pair2, tail_hook=hook2, last_evac=f2_evac)
        for dc in range(NDC):
            SY.wait_ge(sOd[dc].h, 16 * n_chunk)
    return nc


_NC_CACHE = {}


def kernel(L0_attn_x, L0_attn_W_K, L0_attn_W_V, L0_attn_W1, L0_attn_W2,
           L0_attn_m1, L0_attn_m2, L0_ffn_wg1, L0_ffn_bg1, L0_ffn_wg2,
           L0_ffn_bg2, L0_ffn_We, L0_ffn_be):
    n_cores = 8
    x = np.asarray(L0_attn_x, dtype=np.float32)
    c32 = lambda a: np.ascontiguousarray(np.asarray(a, dtype=np.float32))

    feeds_common = {
        "wkt": c32(np.asarray(L0_attn_W_K).T),
        "w1t": c32(np.asarray(L0_attn_W1).T),
        "w2t": c32(np.asarray(L0_attn_W2).T),
        "wg1": c32(L0_ffn_wg1),
        "wg2": c32(L0_ffn_wg2),
        "bg1m": c32(np.asarray(L0_ffn_bg1).reshape(2, 128).T),
        "bg2r": c32(np.asarray(L0_ffn_bg2)[:E].reshape(1, E)),
        "be": c32(L0_ffn_be),
        "we": c32(np.asarray(L0_ffn_We).reshape(E * D, D)),
        "jm": np.full((128, 128), 1.0 / D, dtype=np.float32),
        "onescol": np.ones((128, 1), dtype=np.float32),
        "ones1": np.ones((1, 128), dtype=np.float32),
        "idn": np.eye(128, dtype=np.float32),
        "sel": np.repeat(np.eye(E, dtype=np.float32), 128, axis=1),
        "epsc": np.full((1, 1), EPS, dtype=np.float32),
    }
    in_maps = []
    for i in range(n_cores):
        f = dict(feeds_common)
        f["x_fm"] = c32(x[i].T)
        in_maps.append(f)

    key = TCORE // CHUNK
    if key not in _NC_CACHE:
        _NC_CACHE[key] = build(key)
    nc = _NC_CACHE[key]
    res = run_bass_kernel_spmd(nc, in_maps, core_ids=list(range(n_cores)))
    out = np.stack([np.asarray(r["out_fm"]).T for r in res.results], axis=0)
    return np.ascontiguousarray(out.astype(np.float32))


# version 12
# speedup vs baseline: 1.0131x; 1.0131x over previous
import numpy as np
import concourse.bass as bass
import concourse.mybir as mybir
from concourse.bass_utils import run_bass_kernel_spmd
from contextlib import ExitStack

F32 = mybir.dt.float32
BF16 = mybir.dt.bfloat16
AF = mybir.ActivationFunctionType
ALU = mybir.AluOpType
AX = mybir.AxisListType

D = 1024
NDC = 8
TCORE = 8192
CHUNK = 1024
TG = 512
NTG = CHUNK // TG
NTPT = TG // 128
E = 20
NSLOT = 4
H4 = 256
EPS = 1e-5


class Ctr:
    def __init__(self, h):
        self.h = h
        self.n = 0

    def inc(self, instr, k=1):
        instr.then_inc(self.h, k)
        self.n += k
        return self.n


class Waiter:
    def __init__(self):
        self.seen = {}

    def wait(self, eng, ctr, val):
        if val <= 0:
            return
        key = id(ctr)
        if self.seen.get(key, -1) >= val:
            return
        eng.wait_ge(ctr.h, val)
        self.seen[key] = val


class Bank:
    def __init__(self, t):
        self.t = t
        self.pe_done = 0
        self.released = None


def build(n_chunk):
    nc = bass.Bass("TRN2", target_bir_lowering=False)
    dt = nc.dram_tensor
    x_fm = dt("x_fm", [D, TCORE], F32, kind="ExternalInput").ap()
    wkt = dt("wkt", [D, D], F32, kind="ExternalInput").ap()
    w1t = dt("w1t", [D, 64], F32, kind="ExternalInput").ap()
    w2t = dt("w2t", [64, D], F32, kind="ExternalInput").ap()
    wg1 = dt("wg1", [D, H4], F32, kind="ExternalInput").ap()
    wg2 = dt("wg2", [H4, 40], F32, kind="ExternalInput").ap()
    bg1m = dt("bg1m", [128, 2], F32, kind="ExternalInput").ap()
    bg2r = dt("bg2r", [1, E], F32, kind="ExternalInput").ap()
    be = dt("be", [E, D], F32, kind="ExternalInput").ap()
    we = dt("we", [E * D, D], F32, kind="ExternalInput").ap()
    jm = dt("jm", [128, 128], F32, kind="ExternalInput").ap()
    onescol = dt("onescol", [128, 1], F32, kind="ExternalInput").ap()
    ones1 = dt("ones1", [1, 128], F32, kind="ExternalInput").ap()
    idn = dt("idn", [128, 128], F32, kind="ExternalInput").ap()
    sel = dt("sel", [E, E * 128], F32, kind="ExternalInput").ap()
    epsc = dt("epsc", [1, 1], F32, kind="ExternalInput").ap()
    out_fm = dt("out_fm", [D, TCORE], F32, kind="ExternalOutput").ap()

    we_r = we.rearrange("(e dc p) o -> e p dc o", e=E, dc=NDC, p=128)
    x_r = x_fm.rearrange("(dc p) t -> p dc t", p=128)
    o_r = out_fm.rearrange("(dc p) t -> p dc t", p=128)

    with ExitStack() as es:
        sb = lambda name, shape, dtype: es.enter_context(nc.sbuf_tensor(name, shape, dtype))
        ps = lambda name, shape, dtype: es.enter_context(nc.psum_tensor(name, shape, dtype))

        X = sb("X", [128, NDC * CHUNK], F32)
        ZN = sb("ZN", [128, NDC * CHUNK], BF16)
        SQ = sb("SQ", [128, 2 * NDC * TG], BF16)
        WS = sb("WS", [128, NSLOT * NDC * D], BF16)
        ZE = sb("ZE", [128, 4 * NDC * TG], BF16)
        ROWA = sb("ROWA", [1, 2 * TG], F32)
        ROWB = sb("ROWB", [1, 2 * TG], F32)
        HB = sb("HB", [128, 2 * TG], BF16)
        ABUF = sb("ABUF", [64, TG], BF16)
        GBB = sb("GBB", [128, 3 * CHUNK], BF16)
        MBS = sb("MBS", [128, 2 * TG], BF16)
        RBS = sb("RBS", [128, 2 * TG], BF16)
        GTB = sb("GTB", [E, CHUNK], BF16)
        GBT = sb("GBT", [128, NTG * NTPT * E], BF16)
        PRB = sb("PRB", [128, NTPT * E], F32)
        MSK = sb("MSK", [128, NTPT * E], F32)
        TM2 = sb("TM2", [128, NTPT * E], F32)
        NRM = sb("NRM", [128, NTPT], F32)
        SUMS = sb("SUMS", [128, NTPT], F32)
        RS = sb("RS", [128, NTPT], F32)
        M1 = sb("M1", [128, NTPT], F32)
        W1TB = sb("W1TB", [128, NDC * 64], BF16)
        W2TB = sb("W2TB", [64, D], BF16)
        WG1B = sb("WG1B", [128, NDC * H4], BF16)
        WG2B = sb("WG2B", [128, 2 * E], BF16)
        BEB = sb("BEB", [E, D], BF16)
        BG1C = sb("BG1C", [128, 2], F32)
        BG2R = sb("BG2R", [1, E], BF16)
        JB = sb("JB", [128, 128], BF16)
        OCB = sb("OCB", [128, 1], BF16)
        O1B = sb("O1B", [1, 128], BF16)
        IDNB = sb("IDNB", [128, 128], BF16)
        SELB = sb("SELB", [E, E * 128], BF16)
        EPSB = sb("EPSB", [1, 1], F32)

        banks = [Bank(ps(f"pb{i}", [128, 512], F32)) for i in range(7)]
        pGT = ps("pGT", [E, 128], BF16)
        pgt_pe_done = [0]
        pgt_released = [None]

        sT = Ctr(es.enter_context(nc.semaphore("sT")))
        sV = Ctr(es.enter_context(nc.semaphore("sV")))
        sS = Ctr(es.enter_context(nc.semaphore("sS")))
        sG = Ctr(es.enter_context(nc.semaphore("sG")))
        sW = Ctr(es.enter_context(nc.semaphore("sW")))
        sWS = [Ctr(es.enter_context(nc.semaphore(f"sWS{i}"))) for i in range(NSLOT)]
        sXd = [Ctr(es.enter_context(nc.semaphore(f"sXd{i}"))) for i in range(NDC)]
        sOd = [Ctr(es.enter_context(nc.semaphore(f"sOd{i}"))) for i in range(NDC)]

        blk = es.enter_context(nc.Block())
        T, V, S, G, SY = nc.tensor, nc.vector, nc.scalar, nc.gpsimd, nc.sync
        wt, wv, ws, wg, wy = Waiter(), Waiter(), Waiter(), Waiter(), Waiter()

        def pe_write(bank):
            if bank.released is not None:
                wt.wait(T, bank.released[0], bank.released[1])

        def pe_done(bank, instr):
            bank.pe_done = sT.inc(instr)

        def reader_wait(w, eng, bank):
            w.wait(eng, sT, bank.pe_done)

        def release(bank, ctr, instr, pre=None):
            n = pre if pre is not None else ctr.inc(instr)
            bank.released = (ctr, n)
            return n

        G.dma_start(W1TB[:].rearrange("p (kc d) -> p kc d", kc=NDC),
                    w1t.rearrange("(kc p) d -> p kc d", p=128)).then_inc(sW.h, 16)
        G.dma_start(W2TB[:], w2t).then_inc(sW.h, 16)
        G.dma_start(WG1B[:].rearrange("p (dc h) -> p dc h", dc=NDC),
                    wg1.rearrange("(dc p) h -> p dc h", p=128)).then_inc(sW.h, 16)
        G.dma_start(WG2B[:].rearrange("p (hc e) -> p hc e", hc=2),
                    wg2.rearrange("(hc p) e -> p hc e", p=128)[:, :, 0:E]).then_inc(sW.h, 16)
        G.dma_start(BEB[:], be).then_inc(sW.h, 16)
        G.dma_start(BG2R[:], bg2r).then_inc(sW.h, 16)
        G.dma_start(JB[:], jm).then_inc(sW.h, 16)
        G.dma_start(OCB[:], onescol).then_inc(sW.h, 16)
        G.dma_start(O1B[:], ones1).then_inc(sW.h, 16)
        G.dma_start(IDNB[:], idn).then_inc(sW.h, 16)
        G.dma_start(SELB[:], sel).then_inc(sW.h, 16)
        G.dma_start(BG1C[:], bg1m).then_inc(sW.h, 16)
        G.dma_start(EPSB[:], epsc).then_inc(sW.h, 16)
        sW.n += 16 * 13
        init_w = sW.n

        slots_free = [0] * NSLOT
        slot_loaded = [0] * NSLOT
        slot_rr = [0]

        def issue_weight_load(kind, e=None):
            s = slot_rr[0] % NSLOT
            slot_rr[0] += 1
            wg.wait(G, sT, slots_free[s])
            dst = WS[:, s * NDC * D:(s + 1) * NDC * D].rearrange("p (dc o) -> p dc o", dc=NDC)
            if kind == "we":
                G.dma_start(dst, we_r[e]).then_inc(sWS[s].h, 16)
            else:
                G.dma_start(dst, wkt.rearrange("(hc p) k -> p hc k", p=128)).then_inc(sWS[s].h, 16)
            sWS[s].n += 16
            slot_loaded[s] = sWS[s].n
            return s

        ln_pool_prev = [0, 0]
        rbs_guard = [[], []]

        class LNPhase:

            def __init__(self, x_dep):
                self.x_dep = x_dep
                self.tgb = {0: (banks[2], banks[3]), 1: (banks[4], banks[5])}
                self.cast_a = {}
                self.cast_v = {}
                self.subs_done = {}
                self.sq_done = {}
                self.rstd_done = {}
                self.zn_ready = [0, 0]
                self.done_cast = set()
                self.done = set()

            def cast(self, tg, dc):
                if (tg, dc) in self.done_cast:
                    return
                self.done_cast.add((tg, dc))
                t0 = tg * TG
                sl = slice((tg * NDC + dc) * TG, (tg * NDC + dc + 1) * TG)
                xs = X[:, dc * CHUNK + t0:dc * CHUNK + t0 + TG]
                if dc % 2 == 0:
                    for (ctr, cnt) in self.x_dep(tg, dc):
                        ws.wait(S, ctr, cnt)
                    i = S.activation(SQ[:, sl], xs, AF.Copy)
                    self.cast_a[tg] = sS.inc(i)
                else:
                    for (ctr, cnt) in self.x_dep(tg, dc):
                        wv.wait(V, ctr, cnt)
                    i = V.tensor_copy(SQ[:, sl], xs)
                    if dc == NDC - 1:
                        i = V.drain()
                    self.cast_v[tg] = sV.inc(i)

            def casts(self, tg):
                for dc in range(NDC):
                    self.cast(tg, dc)

            def stats_mean(self, tg):
                if ("stats_mean", tg) in self.done:
                    return
                self.done.add(("stats_mean", tg))
                self.casts(tg)
                bM, bR = self.tgb[tg]
                pe_write(bM)
                wt.wait(T, sS, self.cast_a[tg])
                wt.wait(T, sV, self.cast_v[tg])
                for dc in range(NDC):
                    i = T.matmul(bM.t[:], JB[:],
                                 SQ[:, (tg * NDC + dc) * TG:(tg * NDC + dc + 1) * TG],
                                 start=(dc == 0), stop=(dc == NDC - 1))
                pe_done(bM, i)
                reader_wait(ws, S, bM)
                ws.wait(S, sG, ln_pool_prev[tg])
                for (ctr, cnt) in rbs_guard[tg]:
                    ws.wait(S, ctr, cnt)
                i = S.activation(MBS[:, tg * TG:(tg + 1) * TG], bM.t[:], AF.Copy)
                mbs_done = sS.inc(i)
                bM.released = (sS, mbs_done)
                t0 = tg * TG
                rsl = slice(tg * TG, (tg + 1) * TG)
                wv.wait(V, sS, mbs_done)
                for dc in range(NDC):
                    sl = slice((tg * NDC + dc) * TG, (tg * NDC + dc + 1) * TG)
                    i = V.tensor_tensor(ZN[:, dc * CHUNK + t0:dc * CHUNK + t0 + TG],
                                        SQ[:, sl], MBS[:, rsl], ALU.subtract)
                i = V.drain()
                self.subs_done[tg] = sV.inc(i)

            def stats_s2(self, tg):
                if ("stats_s2", tg) in self.done:
                    return
                self.done.add(("stats_s2", tg))
                self.stats_mean(tg)
                bR = self.tgb[tg][1]
                for dc in range(NDC):
                    sl = slice((tg * NDC + dc) * TG, (tg * NDC + dc + 1) * TG)
                    i = V.tensor_tensor(SQ[:, sl], SQ[:, sl], SQ[:, sl], ALU.mult)
                i = V.drain()
                self.sq_done[tg] = sV.inc(i)
                pe_write(bR)
                wt.wait(T, sV, self.sq_done[tg])
                for dc in range(NDC):
                    i = T.matmul(bR.t[0:1, :], OCB[:],
                                 SQ[:, (tg * NDC + dc) * TG:(tg * NDC + dc + 1) * TG],
                                 start=(dc == 0), stop=(dc == NDC - 1))
                pe_done(bR, i)

            def rows(self, tg):
                if ("rows", tg) in self.done:
                    return
                self.done.add(("rows", tg))
                self.stats_s2(tg)
                bR = self.tgb[tg][1]
                rsl = slice(tg * TG, (tg + 1) * TG)
                reader_wait(ws, S, bR)
                S.drain()
                S.activation(ROWA[0:1, rsl], bR.t[0:1, :], AF.Copy, scale=1.0 / D)
                i = S.activation(ROWB[0:1, rsl], MBS[0:1, rsl], AF.Square)
                rows_done = sS.inc(i)
                wv.wait(V, sS, rows_done)
                V.tensor_tensor(ROWA[0:1, rsl], ROWA[0:1, rsl], ROWB[0:1, rsl], ALU.subtract)
                i = V.drain()
                var_done = sV.inc(i)
                ws.wait(S, sV, var_done)
                S.activation(ROWB[0:1, rsl], ROWA[0:1, rsl], AF.Ln, bias=EPSB[0:1, :])
                S.drain()
                i = S.activation(RBS[0:1, rsl], ROWB[0:1, rsl], AF.Exp, scale=-0.5)
                self.rstd_done[tg] = sS.inc(i)

            def bcast(self, tg):
                if ("bcast", tg) in self.done:
                    return
                self.done.add(("bcast", tg))
                self.rows(tg)
                bR = self.tgb[tg][1]
                rsl = slice(tg * TG, (tg + 1) * TG)
                wt.wait(T, sS, self.rstd_done[tg])
                i = T.matmul(bR.t[:], O1B[:], RBS[0:1, rsl], start=True, stop=True)
                pe_done(bR, i)
                reader_wait(ws, S, bR)
                i = S.activation(RBS[:, rsl], bR.t[:], AF.Copy)
                self.rbs_done = sS.inc(i)
                release(bR, sS, i, pre=self.rbs_done)

            def apply(self, tg):
                if ("apply", tg) in self.done:
                    return
                self.done.add(("apply", tg))
                self.bcast(tg)
                rsl = slice(tg * TG, (tg + 1) * TG)
                t0 = tg * TG
                wv.wait(V, sS, self.rbs_done)
                for dc in range(NDC):
                    i = V.tensor_tensor(ZN[:, dc * CHUNK + t0:dc * CHUNK + t0 + TG],
                                        ZN[:, dc * CHUNK + t0:dc * CHUNK + t0 + TG],
                                        RBS[:, rsl], ALU.mult)
                i = V.drain()
                dve_mul_done = sV.inc(i)
                wg.wait(G, sV, dve_mul_done)
                i = G.engine_nop()
                self.zn_ready[tg] = sG.inc(i)
                ln_pool_prev[tg] = self.zn_ready[tg]
                rbs_guard[tg] = []

            def finish(self):
                for tg in range(NTG):
                    self.apply(tg)
                return self.zn_ready

        def emit_prep(zn_ready, hook=None, stage_hooks=None):
            chains = {}

            def tg_block(tg, shooks):
                t0 = tg * TG
                hb_banks = []
                for hc in range(2):
                    b = banks[hc]
                    pe_write(b)
                    wt.wait(T, sG, zn_ready[tg])
                    for dc in range(NDC):
                        i = T.matmul(b.t[:], WG1B[:, dc * H4 + hc * 128:dc * H4 + hc * 128 + 128],
                                     ZN[:, dc * CHUNK + t0:dc * CHUNK + t0 + TG],
                                     start=(dc == 0), stop=(dc == NDC - 1))
                    pe_done(b, i)
                    hb_banks.append(b)
                for hc in range(2):
                    b = hb_banks[hc]
                    reader_wait(ws, S, b)
                    i = S.activation(HB[:, hc * TG:(hc + 1) * TG], b.t[:], AF.Gelu,
                                     bias=BG1C[:, hc:hc + 1])
                    release(b, sS, i)
                if shooks and "h0" in shooks:
                    shooks["h0"]()
                hb_done = sS.n
                lb = banks[2 + tg]
                pe_write(lb)
                wt.wait(T, sS, hb_done)
                for ts in range(NTPT):
                    T.matmul(lb.t[:, ts * 32:ts * 32 + E],
                             HB[:, 0 * TG + ts * 128:0 * TG + ts * 128 + 128],
                             WG2B[:, 0:E], start=True, stop=False)
                    T.matmul(lb.t[:, ts * 32:ts * 32 + E],
                             HB[:, 1 * TG + ts * 128:1 * TG + ts * 128 + 128],
                             WG2B[:, E:2 * E], start=False, stop=False)
                    i = T.matmul(lb.t[:, ts * 32:ts * 32 + E], O1B[:], BG2R[:],
                                 start=False, stop=True)
                pe_done(lb, i)
                if shooks and "l0" in shooks:
                    shooks["l0"]()
                lb3 = bass.AP(lb.t[:].tensor, 0, [[512, 128], [32, NTPT], [1, E]])
                reader_wait(wv, V, lb)
                V.tensor_reduce(NRM[:], lb3, axis=AX.X, op=ALU.max, negate=True)
                i = V.drain()
                nrm_done = sV.inc(i)
                for ts in range(NTPT):
                    ws.wait(S, sV, nrm_done)
                    i = S.activation(PRB[:, ts * E:(ts + 1) * E], lb.t[:, ts * 32:ts * 32 + E],
                                     AF.Exp, bias=NRM[:, ts:ts + 1], accum_out=SUMS[:, ts:ts + 1])
                ed = sS.inc(i)
                release(lb, sS, i, pre=ed)
                prb3 = PRB[:].rearrange("p (s e) -> p s e", e=E)
                msk3 = MSK[:].rearrange("p (s e) -> p s e", e=E)
                tm23 = TM2[:].rearrange("p (s e) -> p s e", e=E)
                m2b = bass.AP(M1[:].tensor, 0, [[NTPT, 128], [1, NTPT], [0, E]])
                rsb = bass.AP(RS[:].tensor, 0, [[NTPT, 128], [1, NTPT], [0, E]])
                wv.wait(V, sS, ed)
                V.tensor_scalar(MSK[:], PRB[:], 1.0, None, ALU.is_ge)
                V.reciprocal(RS[:], SUMS[:])
                V.drain()
                V.scalar_tensor_tensor(MSK[:], MSK[:], -1e9, PRB[:], ALU.mult, ALU.add)
                V.drain()
                V.tensor_reduce(M1[:], msk3, axis=AX.X, op=ALU.max)
                V.drain()
                V.tensor_tensor(msk3, prb3, m2b, ALU.is_ge)
                V.tensor_tensor(tm23, prb3, rsb, ALU.mult)
                V.drain()
                V.tensor_tensor(GBT[:, tg * NTPT * E:(tg + 1) * NTPT * E], TM2[:], MSK[:],
                                ALU.mult)
                i = V.drain()
                chains[tg] = sV.inc(i)

            def transposes(tg):
                for ts in range(NTPT):
                    if pgt_released[0] is not None:
                        wt.wait(T, pgt_released[0][0], pgt_released[0][1])
                    wt.wait(T, sV, chains[tg])
                    i = T.transpose(pGT[:], GBT[:, (tg * NTPT + ts) * E:(tg * NTPT + ts + 1) * E],
                                    IDNB[:])
                    pgt_pe_done[0] = sT.inc(i)
                    wv.wait(V, sT, pgt_pe_done[0])
                    i = V.tensor_copy(GTB[:, (tg * NTPT + ts) * 128:(tg * NTPT + ts) * 128 + 128],
                                      pGT[:])
                    sV.inc(i)
                    pgt_released[0] = (sV, sV.n)
                i = V.drain()
                sV.inc(i)
                if hook is not None:
                    hook(tg)

            tg_block(0, stage_hooks)
            if stage_hooks and "c0" in stage_hooks:
                stage_hooks["c0"]()
            transposes(0)
            if stage_hooks and "t0" in stage_hooks:
                stage_hooks["t0"]()
            tg_block(1, None)
            transposes(1)

        z_free = [0] * 4
        gbb_free = {}

        class ExpertPhase:

            def __init__(self):
                self.gb_banks = [banks[2], banks[3], banks[4], banks[5]]
                self.zorder = []
                for pair in range(E // 2):
                    for tg in range(NTG):
                        for e in (2 * pair, 2 * pair + 1):
                            self.zorder.append((e, tg))
                self.z_done = {}
                self.z_sl = {}
                self.gbb_req = {}
                self.zq = 0
                self.gb_emitted = set()

            def gb_half(self, pair, tg):
                for j in range(2):
                    e = 2 * pair + j
                    b = self.gb_banks[2 * j + tg]
                    pe_write(b)
                    wt.wait(T, sV, self.gtb_done[tg])
                    i = T.matmul(b.t[:], SELB[:, e * 128:(e + 1) * 128],
                                 GTB[:, tg * TG:(tg + 1) * TG], start=True, stop=True)
                    pe_done(b, i)
                for j in range(2):
                    e = 2 * pair + j
                    zs = e % 3
                    b = self.gb_banks[2 * j + tg]
                    ws.wait(S, sV, gbb_free.get((zs, tg), 0))
                    reader_wait(ws, S, b)
                    i = S.activation(GBB[:, zs * CHUNK + tg * TG:zs * CHUNK + (tg + 1) * TG],
                                     b.t[:], AF.Copy)
                    release(b, sS, i)
                    self.gbb_req[(e, tg)] = sS.n
                self.gb_emitted.add((pair, tg))

            def z_pump(self, pe_reachable):
                while self.zq < len(self.zorder):
                    e, tg = self.zorder[self.zq]
                    if (e, tg) not in self.gbb_req:
                        break
                    slot = self.zq % 4
                    if z_free[slot] is None or z_free[slot] > pe_reachable:
                        break
                    prev_free = z_free[slot]
                    self.zq += 1
                    self.z_sl[(e, tg)] = slot
                    z_free[slot] = None
                    wv.wait(V, sS, self.gbb_req[(e, tg)])
                    wv.wait(V, sT, prev_free)
                    wv.wait(V, sG, self.zn_ready[tg])
                    zs = e % 3
                    for dc in range(NDC):
                        V.tensor_tensor(
                            ZE[:, slot * NDC * TG + dc * TG:slot * NDC * TG + (dc + 1) * TG],
                            ZN[:, dc * CHUNK + tg * TG:dc * CHUNK + tg * TG + TG],
                            GBB[:, zs * CHUNK + tg * TG:zs * CHUNK + (tg + 1) * TG],
                            ALU.mult)
                    i = V.drain()
                    n = sV.inc(i)
                    self.z_done[(e, tg)] = n
                    gbb_free[(zs, tg)] = n

            def prep_hook(self, tg):
                self.gb_half(0, tg)
                self.z_pump(sT.n)

            def run(self, eslots, post_pair=None, tail_hook=None, last_evac=None):
                if last_evac is None:
                    last_evac = {}
                for pair in range(E // 2):
                    e0, e1 = 2 * pair, 2 * pair + 1
                    s0, s1 = eslots[e0], eslots[e1]
                    if pair + 1 < E // 2:
                        for tg in range(NTG):
                            self.gb_half(pair + 1, tg)
                    wt.wait(T, sWS[s0], slot_loaded[s0])
                    wt.wait(T, sWS[s1], slot_loaded[s1])
                    for tg in range(NTG):
                        wt.wait(T, sV, self.z_done[(e0, tg)])
                        wt.wait(T, sV, self.z_done[(e1, tg)])
                        zq0, zq1 = self.z_sl[(e0, tg)], self.z_sl[(e1, tg)]
                        for oc in range(NDC):
                            b = banks[[0, 1, 6][oc % 3]]
                            pe_write(b)
                            first = True
                            if pair == 0:
                                T.matmul(b.t[:], BEB[:, oc * 128:(oc + 1) * 128],
                                         GTB[:, tg * TG:(tg + 1) * TG], start=True, stop=False)
                                first = False
                            for j, (e, s, zz) in enumerate(((e0, s0, zq0), (e1, s1, zq1))):
                                for dc in range(NDC):
                                    i = T.matmul(
                                        b.t[:],
                                        WS[:, s * NDC * D + dc * D + oc * 128:
                                              s * NDC * D + dc * D + oc * 128 + 128],
                                        ZE[:, zz * NDC * TG + dc * TG:zz * NDC * TG + dc * TG + TG],
                                        start=first, stop=(j == 1 and dc == NDC - 1))
                                    first = False
                                    if oc == NDC - 1 and dc == NDC - 1 and j == 0:
                                        z_free[zq0] = sT.inc(i)
                                        if tg == NTG - 1:
                                            slots_free[s0] = z_free[zq0]
                            pe_done(b, i)
                            if oc == NDC - 1:
                                z_free[zq1] = sT.n
                                if tg == NTG - 1:
                                    slots_free[s1] = sT.n
                            stop_n = sT.n
                            wv.wait(V, sT, stop_n)
                            dst = X[:, oc * CHUNK + tg * TG:oc * CHUNK + tg * TG + TG]
                            i = V.scalar_tensor_tensor(dst, b.t[:], 0.5, dst,
                                                       ALU.mult, ALU.add)
                            release(b, sV, i)
                            if pair == E // 2 - 1:
                                i = V.drain()
                                last_evac[(tg, oc)] = sV.inc(i)
                                if tail_hook is not None:
                                    tail_hook(tg, oc)
                            self.z_pump(stop_n)
                    if post_pair is not None:
                        post_pair(pair)
                i = V.drain()
                return sV.inc(i), last_evac

        def emit_attn(wkt_slot, zn_ready, hook=None, att_evac=None, k_hooks=None):
            if att_evac is None:
                att_evac = {}
            k_hooks = k_hooks or {}
            for tg in range(NTG):
                t0 = tg * TG
                kb_dones = []
                for kc in range(NDC):
                    b = banks[[0, 1, 6][kc % 3]]
                    pe_write(b)
                    wt.wait(T, sG, zn_ready[tg])
                    wt.wait(T, sWS[wkt_slot], slot_loaded[wkt_slot])
                    for hc in range(NDC):
                        i = T.matmul(b.t[:],
                                     WS[:, wkt_slot * NDC * D + hc * D + kc * 128:
                                           wkt_slot * NDC * D + hc * D + kc * 128 + 128],
                                     ZN[:, hc * CHUNK + t0:hc * CHUNK + t0 + TG],
                                     start=(hc == 0), stop=(hc == NDC - 1))
                    pe_done(b, i)
                    reader_wait(ws, S, b)
                    i = S.activation(SQ[:, (tg * NDC + kc) * TG:(tg * NDC + kc + 1) * TG],
                                     b.t[:], AF.Copy)
                    release(b, sS, i)
                    kb_dones.append(sS.n)
                    if tg == 0 and kc in k_hooks:
                        k_hooks[kc]()
                b = banks[4]
                pe_write(b)
                for kc in range(NDC):
                    wt.wait(T, sS, kb_dones[kc])
                    i = T.matmul(b.t[0:64, :], W1TB[:, kc * 64:(kc + 1) * 64],
                                 SQ[:, (tg * NDC + kc) * TG:(tg * NDC + kc + 1) * TG],
                                 start=(kc == 0), stop=(kc == NDC - 1))
                pe_done(b, i)
                reader_wait(ws, S, b)
                i = S.activation(ABUF[:], b.t[0:64, :], AF.Gelu)
                ab_done = sS.inc(i)
                release(b, sS, i, pre=ab_done)
                for hc in range(NDC):
                    b = banks[[0, 1, 6][hc % 3]]
                    pe_write(b)
                    wt.wait(T, sS, ab_done)
                    i = T.matmul(b.t[:], W2TB[:, hc * 128:(hc + 1) * 128], ABUF[:],
                                 start=True, stop=True)
                    pe_done(b, i)
                    reader_wait(wv, V, b)
                    i = V.tensor_tensor(X[:, hc * CHUNK + t0:hc * CHUNK + t0 + TG],
                                        X[:, hc * CHUNK + t0:hc * CHUNK + t0 + TG],
                                        b.t[:], ALU.add)
                    release(b, sV, i)
                    i = V.drain()
                    att_evac[(tg, hc)] = sV.inc(i)
                    if hook is not None:
                        hook(tg, hc)
            return sV.n, att_evac

        pending_ln1 = [None]

        def mk_ln_hook(ln, extra=None):
            def hook(tg, idx):
                if extra is not None:
                    extra(tg, idx)
                ln.cast(tg, idx)
                if tg == 1:
                    if idx == 0:
                        ln.stats_mean(0)
                    elif idx == 4:
                        ln.stats_s2(0)
                    elif idx == 5:
                        ln.rows(0)
                    elif idx == 6:
                        ln.bcast(0)
                    elif idx == 7:
                        ln.apply(0)
            return hook

        for c in range(n_chunk):
            if pending_ln1[0] is None:
                x_here = []
                for dc in range(NDC):
                    SY.dma_start(X[:, dc * CHUNK:(dc + 1) * CHUNK],
                                 x_r[:, dc, c * CHUNK:(c + 1) * CHUNK]).then_inc(sXd[dc].h, 16)
                    sXd[dc].n += 16
                    x_here.append(sXd[dc].n)
                xh = list(x_here)
                ln1 = LNPhase(lambda tg, dc: [(sXd[dc], xh[dc])])
            else:
                ln1 = pending_ln1[0]
            if c == 0:
                ws.wait(S, sW, init_w)
                wt.wait(T, sW, init_w)
                wv.wait(V, sW, init_w)
                wg.wait(G, sW, init_w)
            ln1.apply(0)

            my1 = {e: issue_weight_load("we", e) for e in range(4)}
            ep1 = ExpertPhase()
            ep1.zn_ready = ln1.zn_ready
            ep1.gtb_done = [0, 0]
            ln1.stats_mean(1)
            emit_prep(ln1.zn_ready,
                      hook=lambda tg: (ep1.gtb_done.__setitem__(tg, sV.n),
                                       ep1.prep_hook(tg)),
                      stage_hooks={"h0": lambda: ln1.stats_s2(1),
                                   "l0": lambda: ln1.rows(1),
                                   "c0": lambda: ln1.bcast(1),
                                   "t0": lambda: ln1.apply(1)})
            f2pre = {}

            def post_pair1(p):
                for e in (4 + 2 * p, 5 + 2 * p):
                    if e < E:
                        my1[e] = issue_weight_load("we", e)
                if p == 8:
                    f2pre["wkt"] = issue_weight_load("wkt")
                if p == 9:
                    f2pre[0] = issue_weight_load("we", 0)
                    f2pre[1] = issue_weight_load("we", 1)

            f1_evac = {}
            ln2 = LNPhase(lambda tg, dc: [(sV, f1_evac[(tg, dc)])])
            ep1.run(my1, post_pair1, tail_hook=mk_ln_hook(ln2), last_evac=f1_evac)
            ln2.stats_mean(1)
            att_evac = {}
            ln3 = LNPhase(lambda tg, dc: [(sV, att_evac[(tg, dc)])])
            emit_attn(f2pre["wkt"], ln2.zn_ready, hook=mk_ln_hook(ln3), att_evac=att_evac,
                      k_hooks={1: lambda: ln2.stats_s2(1),
                               3: lambda: ln2.rows(1),
                               5: lambda: ln2.bcast(1),
                               6: lambda: ln2.apply(1)})
            slots_free[f2pre["wkt"]] = sT.n
            f2pre[2] = issue_weight_load("we", 2)
            f2pre[3] = issue_weight_load("we", 3)
            ln3.apply(0)
            ln3.stats_mean(1)
            ep2 = ExpertPhase()
            ep2.zn_ready = ln3.zn_ready
            ep2.gtb_done = [0, 0]
            emit_prep(ln3.zn_ready,
                      hook=lambda tg: (ep2.gtb_done.__setitem__(tg, sV.n),
                                       ep2.prep_hook(tg)),
                      stage_hooks={"h0": lambda: ln3.stats_s2(1),
                                   "l0": lambda: ln3.rows(1),
                                   "c0": lambda: ln3.bcast(1),
                                   "t0": lambda: ln3.apply(1)})
            my2 = {e: f2pre[e] for e in range(4)}

            def post_pair2(p):
                for e in (4 + 2 * p, 5 + 2 * p):
                    if e < E:
                        my2[e] = issue_weight_load("we", e)

            f2_evac = {}
            nxt_here = {}

            def store_load(tg, dc):
                hsl = slice(dc * CHUNK + tg * TG, dc * CHUNK + (tg + 1) * TG)
                wy.wait(SY, sV, f2_evac[(tg, dc)])
                wy.wait(SY, sOd[dc], sOd[dc].n)
                wy.wait(SY, sXd[dc], sXd[dc].n)
                SY.dma_start(o_r[:, dc, c * CHUNK + tg * TG:c * CHUNK + (tg + 1) * TG],
                             X[:, hsl]).then_inc(sOd[dc].h, 16)
                sOd[dc].n += 16
                if c + 1 < n_chunk:
                    wy.wait(SY, sOd[dc], sOd[dc].n)
                    SY.dma_start(X[:, hsl],
                                 x_r[:, dc, (c + 1) * CHUNK + tg * TG:
                                            (c + 1) * CHUNK + (tg + 1) * TG]).then_inc(sXd[dc].h, 16)
                    sXd[dc].n += 16
                    nxt_here[(tg, dc)] = sXd[dc].n

            if c + 1 < n_chunk:
                lnext = LNPhase(lambda tg, dc: [(sXd[dc], nxt_here[(tg, dc)])])
                hook2 = mk_ln_hook(lnext, extra=store_load)
                pending_ln1[0] = lnext
            else:
                def hook2(tg, dc):
                    store_load(tg, dc)
                pending_ln1[0] = None
            ep2.run(my2, post_pair2, tail_hook=hook2, last_evac=f2_evac)
        for dc in range(NDC):
            SY.wait_ge(sOd[dc].h, 16 * n_chunk)
    return nc


_NC_CACHE = {}


def kernel(L0_attn_x, L0_attn_W_K, L0_attn_W_V, L0_attn_W1, L0_attn_W2,
           L0_attn_m1, L0_attn_m2, L0_ffn_wg1, L0_ffn_bg1, L0_ffn_wg2,
           L0_ffn_bg2, L0_ffn_We, L0_ffn_be):
    n_cores = 8
    x = np.asarray(L0_attn_x, dtype=np.float32)
    c32 = lambda a: np.ascontiguousarray(np.asarray(a, dtype=np.float32))

    feeds_common = {
        "wkt": c32(np.asarray(L0_attn_W_K).T),
        "w1t": c32(np.asarray(L0_attn_W1).T),
        "w2t": c32(np.asarray(L0_attn_W2).T),
        "wg1": c32(L0_ffn_wg1),
        "wg2": c32(L0_ffn_wg2),
        "bg1m": c32(np.asarray(L0_ffn_bg1).reshape(2, 128).T),
        "bg2r": c32(np.asarray(L0_ffn_bg2)[:E].reshape(1, E)),
        "be": c32(L0_ffn_be),
        "we": c32(np.asarray(L0_ffn_We).reshape(E * D, D)),
        "jm": np.full((128, 128), 1.0 / D, dtype=np.float32),
        "onescol": np.ones((128, 1), dtype=np.float32),
        "ones1": np.ones((1, 128), dtype=np.float32),
        "idn": np.eye(128, dtype=np.float32),
        "sel": np.repeat(np.eye(E, dtype=np.float32), 128, axis=1),
        "epsc": np.full((1, 1), EPS, dtype=np.float32),
    }
    in_maps = []
    for i in range(n_cores):
        f = dict(feeds_common)
        f["x_fm"] = c32(x[i].T)
        in_maps.append(f)

    key = TCORE // CHUNK
    if key not in _NC_CACHE:
        _NC_CACHE[key] = build(key)
    nc = _NC_CACHE[key]
    res = run_bass_kernel_spmd(nc, in_maps, core_ids=list(range(n_cores)))
    out = np.stack([np.asarray(r["out_fm"]).T for r in res.results], axis=0)
    return np.ascontiguousarray(out.astype(np.float32))


# version 13
# speedup vs baseline: 1.0198x; 1.0066x over previous
import numpy as np
import concourse.bass as bass
import concourse.mybir as mybir
from concourse.bass_utils import run_bass_kernel_spmd
from contextlib import ExitStack

F32 = mybir.dt.float32
BF16 = mybir.dt.bfloat16
AF = mybir.ActivationFunctionType
ALU = mybir.AluOpType
AX = mybir.AxisListType

D = 1024
NDC = 8
TCORE = 8192
CHUNK = 1024
TG = 512
NTG = CHUNK // TG
NTPT = TG // 128
E = 20
NSLOT = 4
H4 = 256
EPS = 1e-5


class Ctr:
    def __init__(self, h):
        self.h = h
        self.n = 0

    def inc(self, instr, k=1):
        instr.then_inc(self.h, k)
        self.n += k
        return self.n


class Waiter:
    def __init__(self):
        self.seen = {}

    def wait(self, eng, ctr, val):
        if val <= 0:
            return
        key = id(ctr)
        if self.seen.get(key, -1) >= val:
            return
        eng.wait_ge(ctr.h, val)
        self.seen[key] = val


class Bank:
    def __init__(self, t):
        self.t = t
        self.pe_done = 0
        self.released = None


def build(n_chunk):
    nc = bass.Bass("TRN2", target_bir_lowering=False)
    dt = nc.dram_tensor
    x_fm = dt("x_fm", [D, TCORE], F32, kind="ExternalInput").ap()
    wkt = dt("wkt", [D, D], F32, kind="ExternalInput").ap()
    w1t = dt("w1t", [D, 64], F32, kind="ExternalInput").ap()
    w2t = dt("w2t", [64, D], F32, kind="ExternalInput").ap()
    wg1 = dt("wg1", [D, H4], F32, kind="ExternalInput").ap()
    wg2 = dt("wg2", [H4, 40], F32, kind="ExternalInput").ap()
    bg1m = dt("bg1m", [128, 2], F32, kind="ExternalInput").ap()
    bg2r = dt("bg2r", [1, E], F32, kind="ExternalInput").ap()
    be = dt("be", [E, D], F32, kind="ExternalInput").ap()
    we = dt("we", [E * D, D], F32, kind="ExternalInput").ap()
    jm = dt("jm", [128, 128], F32, kind="ExternalInput").ap()
    onescol = dt("onescol", [128, 1], F32, kind="ExternalInput").ap()
    ones1 = dt("ones1", [1, 128], F32, kind="ExternalInput").ap()
    idn = dt("idn", [128, 128], F32, kind="ExternalInput").ap()
    sel = dt("sel", [E, E * 128], F32, kind="ExternalInput").ap()
    epsc = dt("epsc", [1, 1], F32, kind="ExternalInput").ap()
    out_fm = dt("out_fm", [D, TCORE], F32, kind="ExternalOutput").ap()

    we_r = we.rearrange("(e dc p) o -> e p dc o", e=E, dc=NDC, p=128)
    x_r = x_fm.rearrange("(dc p) t -> p dc t", p=128)
    o_r = out_fm.rearrange("(dc p) t -> p dc t", p=128)

    with ExitStack() as es:
        sb = lambda name, shape, dtype: es.enter_context(nc.sbuf_tensor(name, shape, dtype))
        ps = lambda name, shape, dtype: es.enter_context(nc.psum_tensor(name, shape, dtype))

        X = sb("X", [128, NDC * CHUNK], F32)
        ZN = sb("ZN", [128, NDC * CHUNK], BF16)
        SQ = sb("SQ", [128, 2 * NDC * TG], BF16)
        WS = sb("WS", [128, NSLOT * NDC * D], BF16)
        ZE = sb("ZE", [128, 4 * NDC * TG], BF16)
        ROWA = sb("ROWA", [1, 2 * TG], F32)
        ROWB = sb("ROWB", [1, 2 * TG], F32)
        HB = sb("HB", [128, 2 * TG], BF16)
        ABUF = sb("ABUF", [64, TG], BF16)
        GBB = sb("GBB", [128, 3 * CHUNK], BF16)
        MBS = sb("MBS", [128, 2 * TG], BF16)
        RBS = sb("RBS", [128, 2 * TG], BF16)
        GTB = sb("GTB", [E, CHUNK], BF16)
        GBT = sb("GBT", [128, NTG * NTPT * E], BF16)
        PRB = sb("PRB", [128, NTPT * E], F32)
        MSK = sb("MSK", [128, NTPT * E], F32)
        TM2 = sb("TM2", [128, NTPT * E], F32)
        NRM = sb("NRM", [128, NTPT], F32)
        SUMS = sb("SUMS", [128, NTPT], F32)
        RS = sb("RS", [128, NTPT], F32)
        M1 = sb("M1", [128, NTPT], F32)
        W1TB = sb("W1TB", [128, NDC * 64], BF16)
        W2TB = sb("W2TB", [64, D], BF16)
        WG1B = sb("WG1B", [128, NDC * H4], BF16)
        WG2B = sb("WG2B", [128, 2 * E], BF16)
        BEB = sb("BEB", [E, D], BF16)
        BG1C = sb("BG1C", [128, 2], F32)
        BG2R = sb("BG2R", [1, E], BF16)
        JB = sb("JB", [128, 128], BF16)
        OCB = sb("OCB", [128, 1], BF16)
        O1B = sb("O1B", [1, 128], BF16)
        IDNB = sb("IDNB", [128, 128], BF16)
        SELB = sb("SELB", [E, E * 128], BF16)
        EPSB = sb("EPSB", [1, 1], F32)

        banks = [Bank(ps(f"pb{i}", [128, 512], F32)) for i in range(7)]
        pGT = ps("pGT", [E, 128], BF16)
        pgt_pe_done = [0]
        pgt_released = [None]

        sT = Ctr(es.enter_context(nc.semaphore("sT")))
        sV = Ctr(es.enter_context(nc.semaphore("sV")))
        sS = Ctr(es.enter_context(nc.semaphore("sS")))
        sG = Ctr(es.enter_context(nc.semaphore("sG")))
        sW = Ctr(es.enter_context(nc.semaphore("sW")))
        sWS = [Ctr(es.enter_context(nc.semaphore(f"sWS{i}"))) for i in range(NSLOT)]
        sXd = [Ctr(es.enter_context(nc.semaphore(f"sXd{i}"))) for i in range(NDC)]
        sOd = [Ctr(es.enter_context(nc.semaphore(f"sOd{i}"))) for i in range(NDC)]

        blk = es.enter_context(nc.Block())
        T, V, S, G, SY = nc.tensor, nc.vector, nc.scalar, nc.gpsimd, nc.sync
        wt, wv, ws, wg, wy = Waiter(), Waiter(), Waiter(), Waiter(), Waiter()

        def pe_write(bank):
            if bank.released is not None:
                wt.wait(T, bank.released[0], bank.released[1])

        def pe_done(bank, instr):
            bank.pe_done = sT.inc(instr)

        def reader_wait(w, eng, bank):
            w.wait(eng, sT, bank.pe_done)

        def release(bank, ctr, instr, pre=None):
            n = pre if pre is not None else ctr.inc(instr)
            bank.released = (ctr, n)
            return n

        G.dma_start(W1TB[:].rearrange("p (kc d) -> p kc d", kc=NDC),
                    w1t.rearrange("(kc p) d -> p kc d", p=128)).then_inc(sW.h, 16)
        G.dma_start(W2TB[:], w2t).then_inc(sW.h, 16)
        G.dma_start(WG1B[:].rearrange("p (dc h) -> p dc h", dc=NDC),
                    wg1.rearrange("(dc p) h -> p dc h", p=128)).then_inc(sW.h, 16)
        G.dma_start(WG2B[:].rearrange("p (hc e) -> p hc e", hc=2),
                    wg2.rearrange("(hc p) e -> p hc e", p=128)[:, :, 0:E]).then_inc(sW.h, 16)
        G.dma_start(BEB[:], be).then_inc(sW.h, 16)
        G.dma_start(BG2R[:], bg2r).then_inc(sW.h, 16)
        G.dma_start(JB[:], jm).then_inc(sW.h, 16)
        G.dma_start(OCB[:], onescol).then_inc(sW.h, 16)
        G.dma_start(O1B[:], ones1).then_inc(sW.h, 16)
        G.dma_start(IDNB[:], idn).then_inc(sW.h, 16)
        G.dma_start(SELB[:], sel).then_inc(sW.h, 16)
        G.dma_start(BG1C[:], bg1m).then_inc(sW.h, 16)
        G.dma_start(EPSB[:], epsc).then_inc(sW.h, 16)
        sW.n += 16 * 13
        init_w = sW.n

        slots_free = [0] * NSLOT
        slot_loaded = [0] * NSLOT
        slot_rr = [0]

        def issue_weight_load(kind, e=None):
            s = slot_rr[0] % NSLOT
            slot_rr[0] += 1
            wg.wait(G, sT, slots_free[s])
            dst = WS[:, s * NDC * D:(s + 1) * NDC * D].rearrange("p (dc o) -> p dc o", dc=NDC)
            if kind == "we":
                G.dma_start(dst, we_r[e]).then_inc(sWS[s].h, 16)
            else:
                G.dma_start(dst, wkt.rearrange("(hc p) k -> p hc k", p=128)).then_inc(sWS[s].h, 16)
            sWS[s].n += 16
            slot_loaded[s] = sWS[s].n
            return s

        ln_pool_prev = [0, 0]
        rbs_guard = [[], []]

        class LNPhase:

            def __init__(self, x_dep):
                self.x_dep = x_dep
                self.tgb = {0: (banks[2], banks[3]), 1: (banks[4], banks[5])}
                self.cast_a = {}
                self.cast_v = {}
                self.subs_done = {}
                self.sq_done = {}
                self.rstd_done = {}
                self.zn_ready = [0, 0]
                self.done_cast = set()
                self.done = set()

            def cast(self, tg, dc):
                if (tg, dc) in self.done_cast:
                    return
                self.done_cast.add((tg, dc))
                t0 = tg * TG
                sl = slice((tg * NDC + dc) * TG, (tg * NDC + dc + 1) * TG)
                xs = X[:, dc * CHUNK + t0:dc * CHUNK + t0 + TG]
                if dc % 2 == 0:
                    for (ctr, cnt) in self.x_dep(tg, dc):
                        ws.wait(S, ctr, cnt)
                    i = S.activation(SQ[:, sl], xs, AF.Copy)
                    self.cast_a[tg] = sS.inc(i)
                else:
                    for (ctr, cnt) in self.x_dep(tg, dc):
                        wv.wait(V, ctr, cnt)
                    i = V.tensor_copy(SQ[:, sl], xs)
                    if dc == NDC - 1:
                        i = V.drain()
                    self.cast_v[tg] = sV.inc(i)

            def casts(self, tg):
                for dc in range(NDC):
                    self.cast(tg, dc)

            def stats_mean(self, tg):
                if ("stats_mean", tg) in self.done:
                    return
                self.done.add(("stats_mean", tg))
                self.casts(tg)
                bM, bR = self.tgb[tg]
                pe_write(bM)
                wt.wait(T, sS, self.cast_a[tg])
                wt.wait(T, sV, self.cast_v[tg])
                for dc in range(NDC):
                    i = T.matmul(bM.t[:], JB[:],
                                 SQ[:, (tg * NDC + dc) * TG:(tg * NDC + dc + 1) * TG],
                                 start=(dc == 0), stop=(dc == NDC - 1))
                pe_done(bM, i)
                reader_wait(ws, S, bM)
                ws.wait(S, sG, ln_pool_prev[tg])
                for (ctr, cnt) in rbs_guard[tg]:
                    ws.wait(S, ctr, cnt)
                i = S.activation(MBS[:, tg * TG:(tg + 1) * TG], bM.t[:], AF.Copy)
                mbs_done = sS.inc(i)
                bM.released = (sS, mbs_done)
                t0 = tg * TG
                rsl = slice(tg * TG, (tg + 1) * TG)
                wv.wait(V, sS, mbs_done)
                zn3 = bass.AP(ZN[:].tensor, t0,
                              [[NDC * CHUNK, 128], [CHUNK, NDC], [1, TG]])
                sq3 = bass.AP(SQ[:].tensor, tg * NDC * TG,
                              [[2 * NDC * TG, 128], [TG, NDC], [1, TG]])
                mb3 = bass.AP(MBS[:].tensor, tg * TG,
                              [[2 * TG, 128], [0, NDC], [1, TG]])
                V.tensor_tensor(zn3, sq3, mb3, ALU.subtract)
                i = V.drain()
                self.subs_done[tg] = sV.inc(i)

            def stats_s2(self, tg):
                if ("stats_s2", tg) in self.done:
                    return
                self.done.add(("stats_s2", tg))
                self.stats_mean(tg)
                bR = self.tgb[tg][1]
                ssl = slice(tg * NDC * TG, (tg + 1) * NDC * TG)
                i = V.tensor_tensor(SQ[:, ssl], SQ[:, ssl], SQ[:, ssl], ALU.mult)
                i = V.drain()
                self.sq_done[tg] = sV.inc(i)
                pe_write(bR)
                wt.wait(T, sV, self.sq_done[tg])
                for dc in range(NDC):
                    i = T.matmul(bR.t[0:1, :], OCB[:],
                                 SQ[:, (tg * NDC + dc) * TG:(tg * NDC + dc + 1) * TG],
                                 start=(dc == 0), stop=(dc == NDC - 1))
                pe_done(bR, i)

            def rows(self, tg):
                if ("rows", tg) in self.done:
                    return
                self.done.add(("rows", tg))
                self.stats_s2(tg)
                bR = self.tgb[tg][1]
                rsl = slice(tg * TG, (tg + 1) * TG)
                reader_wait(ws, S, bR)
                S.drain()
                S.activation(ROWA[0:1, rsl], bR.t[0:1, :], AF.Copy, scale=1.0 / D)
                i = S.activation(ROWB[0:1, rsl], MBS[0:1, rsl], AF.Square)
                rows_done = sS.inc(i)
                wv.wait(V, sS, rows_done)
                V.tensor_tensor(ROWA[0:1, rsl], ROWA[0:1, rsl], ROWB[0:1, rsl], ALU.subtract)
                i = V.drain()
                var_done = sV.inc(i)
                ws.wait(S, sV, var_done)
                S.activation(ROWB[0:1, rsl], ROWA[0:1, rsl], AF.Ln, bias=EPSB[0:1, :])
                S.drain()
                i = S.activation(RBS[0:1, rsl], ROWB[0:1, rsl], AF.Exp, scale=-0.5)
                self.rstd_done[tg] = sS.inc(i)

            def bcast(self, tg):
                if ("bcast", tg) in self.done:
                    return
                self.done.add(("bcast", tg))
                self.rows(tg)
                bR = self.tgb[tg][1]
                rsl = slice(tg * TG, (tg + 1) * TG)
                wt.wait(T, sS, self.rstd_done[tg])
                i = T.matmul(bR.t[:], O1B[:], RBS[0:1, rsl], start=True, stop=True)
                pe_done(bR, i)
                reader_wait(ws, S, bR)
                i = S.activation(RBS[:, rsl], bR.t[:], AF.Copy)
                self.rbs_done = sS.inc(i)
                release(bR, sS, i, pre=self.rbs_done)

            def apply(self, tg):
                if ("apply", tg) in self.done:
                    return
                self.done.add(("apply", tg))
                self.bcast(tg)
                rsl = slice(tg * TG, (tg + 1) * TG)
                t0 = tg * TG
                wv.wait(V, sS, self.rbs_done)
                zn3 = bass.AP(ZN[:].tensor, t0,
                              [[NDC * CHUNK, 128], [CHUNK, NDC], [1, TG]])
                rb3 = bass.AP(RBS[:].tensor, tg * TG,
                              [[2 * TG, 128], [0, NDC], [1, TG]])
                V.tensor_tensor(zn3, zn3, rb3, ALU.mult)
                i = V.drain()
                dve_mul_done = sV.inc(i)
                wg.wait(G, sV, dve_mul_done)
                i = G.engine_nop()
                self.zn_ready[tg] = sG.inc(i)
                ln_pool_prev[tg] = self.zn_ready[tg]
                rbs_guard[tg] = []

            def finish(self):
                for tg in range(NTG):
                    self.apply(tg)
                return self.zn_ready

        def emit_prep(zn_ready, hook=None, stage_hooks=None):
            chains = {}

            def tg_block(tg, shooks):
                t0 = tg * TG
                hb_banks = []
                for hc in range(2):
                    b = banks[hc]
                    pe_write(b)
                    wt.wait(T, sG, zn_ready[tg])
                    for dc in range(NDC):
                        i = T.matmul(b.t[:], WG1B[:, dc * H4 + hc * 128:dc * H4 + hc * 128 + 128],
                                     ZN[:, dc * CHUNK + t0:dc * CHUNK + t0 + TG],
                                     start=(dc == 0), stop=(dc == NDC - 1))
                    pe_done(b, i)
                    hb_banks.append(b)
                for hc in range(2):
                    b = hb_banks[hc]
                    reader_wait(ws, S, b)
                    i = S.activation(HB[:, hc * TG:(hc + 1) * TG], b.t[:], AF.Gelu,
                                     bias=BG1C[:, hc:hc + 1])
                    release(b, sS, i)
                if shooks and "h0" in shooks:
                    shooks["h0"]()
                hb_done = sS.n
                lb = banks[2 + tg]
                pe_write(lb)
                wt.wait(T, sS, hb_done)
                for ts in range(NTPT):
                    T.matmul(lb.t[:, ts * 32:ts * 32 + E],
                             HB[:, 0 * TG + ts * 128:0 * TG + ts * 128 + 128],
                             WG2B[:, 0:E], start=True, stop=False)
                    T.matmul(lb.t[:, ts * 32:ts * 32 + E],
                             HB[:, 1 * TG + ts * 128:1 * TG + ts * 128 + 128],
                             WG2B[:, E:2 * E], start=False, stop=False)
                    i = T.matmul(lb.t[:, ts * 32:ts * 32 + E], O1B[:], BG2R[:],
                                 start=False, stop=True)
                pe_done(lb, i)
                if shooks and "l0" in shooks:
                    shooks["l0"]()
                lb3 = bass.AP(lb.t[:].tensor, 0, [[512, 128], [32, NTPT], [1, E]])
                reader_wait(wv, V, lb)
                V.tensor_reduce(NRM[:], lb3, axis=AX.X, op=ALU.max, negate=True)
                i = V.drain()
                nrm_done = sV.inc(i)
                for ts in range(NTPT):
                    ws.wait(S, sV, nrm_done)
                    i = S.activation(PRB[:, ts * E:(ts + 1) * E], lb.t[:, ts * 32:ts * 32 + E],
                                     AF.Exp, bias=NRM[:, ts:ts + 1], accum_out=SUMS[:, ts:ts + 1])
                ed = sS.inc(i)
                release(lb, sS, i, pre=ed)
                prb3 = PRB[:].rearrange("p (s e) -> p s e", e=E)
                msk3 = MSK[:].rearrange("p (s e) -> p s e", e=E)
                tm23 = TM2[:].rearrange("p (s e) -> p s e", e=E)
                m2b = bass.AP(M1[:].tensor, 0, [[NTPT, 128], [1, NTPT], [0, E]])
                rsb = bass.AP(RS[:].tensor, 0, [[NTPT, 128], [1, NTPT], [0, E]])
                wv.wait(V, sS, ed)
                V.tensor_scalar(MSK[:], PRB[:], 1.0, None, ALU.is_ge)
                V.reciprocal(RS[:], SUMS[:])
                V.drain()
                V.scalar_tensor_tensor(MSK[:], MSK[:], -1e9, PRB[:], ALU.mult, ALU.add)
                V.drain()
                V.tensor_reduce(M1[:], msk3, axis=AX.X, op=ALU.max)
                V.drain()
                V.tensor_tensor(msk3, prb3, m2b, ALU.is_ge)
                V.tensor_tensor(tm23, prb3, rsb, ALU.mult)
                V.drain()
                V.tensor_tensor(GBT[:, tg * NTPT * E:(tg + 1) * NTPT * E], TM2[:], MSK[:],
                                ALU.mult)
                i = V.drain()
                chains[tg] = sV.inc(i)

            def transposes(tg):
                for ts in range(NTPT):
                    if pgt_released[0] is not None:
                        wt.wait(T, pgt_released[0][0], pgt_released[0][1])
                    wt.wait(T, sV, chains[tg])
                    i = T.transpose(pGT[:], GBT[:, (tg * NTPT + ts) * E:(tg * NTPT + ts + 1) * E],
                                    IDNB[:])
                    pgt_pe_done[0] = sT.inc(i)
                    wv.wait(V, sT, pgt_pe_done[0])
                    i = V.tensor_copy(GTB[:, (tg * NTPT + ts) * 128:(tg * NTPT + ts) * 128 + 128],
                                      pGT[:])
                    sV.inc(i)
                    pgt_released[0] = (sV, sV.n)
                i = V.drain()
                sV.inc(i)
                if hook is not None:
                    hook(tg)

            tg_block(0, stage_hooks)
            if stage_hooks and "c0" in stage_hooks:
                stage_hooks["c0"]()
            transposes(0)
            if stage_hooks and "t0" in stage_hooks:
                stage_hooks["t0"]()
            tg_block(1, None)
            transposes(1)

        z_free = [0] * 4
        gbb_free = {}

        class ExpertPhase:

            def __init__(self):
                self.gb_banks = [banks[2], banks[3], banks[4], banks[5]]
                self.zorder = []
                for pair in range(E // 2):
                    for tg in range(NTG):
                        for e in (2 * pair, 2 * pair + 1):
                            self.zorder.append((e, tg))
                self.z_done = {}
                self.z_sl = {}
                self.gbb_req = {}
                self.zq = 0
                self.gb_emitted = set()

            def gb_half(self, pair, tg):
                for j in range(2):
                    e = 2 * pair + j
                    b = self.gb_banks[2 * j + tg]
                    pe_write(b)
                    wt.wait(T, sV, self.gtb_done[tg])
                    i = T.matmul(b.t[:], SELB[:, e * 128:(e + 1) * 128],
                                 GTB[:, tg * TG:(tg + 1) * TG], start=True, stop=True)
                    pe_done(b, i)
                for j in range(2):
                    e = 2 * pair + j
                    zs = e % 3
                    b = self.gb_banks[2 * j + tg]
                    ws.wait(S, sV, gbb_free.get((zs, tg), 0))
                    reader_wait(ws, S, b)
                    i = S.activation(GBB[:, zs * CHUNK + tg * TG:zs * CHUNK + (tg + 1) * TG],
                                     b.t[:], AF.Copy)
                    release(b, sS, i)
                    self.gbb_req[(e, tg)] = sS.n
                self.gb_emitted.add((pair, tg))

            def z_pump(self, pe_reachable):
                while self.zq < len(self.zorder):
                    e, tg = self.zorder[self.zq]
                    if (e, tg) not in self.gbb_req:
                        break
                    slot = self.zq % 4
                    if z_free[slot] is None or z_free[slot] > pe_reachable:
                        break
                    prev_free = z_free[slot]
                    self.zq += 1
                    self.z_sl[(e, tg)] = slot
                    z_free[slot] = None
                    wv.wait(V, sS, self.gbb_req[(e, tg)])
                    wv.wait(V, sT, prev_free)
                    wv.wait(V, sG, self.zn_ready[tg])
                    zs = e % 3
                    ze3 = bass.AP(ZE[:].tensor, slot * NDC * TG,
                                  [[4 * NDC * TG, 128], [TG, NDC], [1, TG]])
                    zn3 = bass.AP(ZN[:].tensor, tg * TG,
                                  [[NDC * CHUNK, 128], [CHUNK, NDC], [1, TG]])
                    gb3 = bass.AP(GBB[:].tensor, zs * CHUNK + tg * TG,
                                  [[3 * CHUNK, 128], [0, NDC], [1, TG]])
                    V.tensor_tensor(ze3, zn3, gb3, ALU.mult)
                    i = V.drain()
                    n = sV.inc(i)
                    self.z_done[(e, tg)] = n
                    gbb_free[(zs, tg)] = n

            def prep_hook(self, tg):
                self.gb_half(0, tg)
                self.z_pump(sT.n)

            def run(self, eslots, post_pair=None, tail_hook=None, last_evac=None):
                if last_evac is None:
                    last_evac = {}
                for pair in range(E // 2):
                    e0, e1 = 2 * pair, 2 * pair + 1
                    s0, s1 = eslots[e0], eslots[e1]
                    if pair + 1 < E // 2:
                        for tg in range(NTG):
                            self.gb_half(pair + 1, tg)
                    wt.wait(T, sWS[s0], slot_loaded[s0])
                    wt.wait(T, sWS[s1], slot_loaded[s1])
                    for tg in range(NTG):
                        wt.wait(T, sV, self.z_done[(e0, tg)])
                        wt.wait(T, sV, self.z_done[(e1, tg)])
                        zq0, zq1 = self.z_sl[(e0, tg)], self.z_sl[(e1, tg)]
                        for oc in range(NDC):
                            b = banks[[0, 1, 6][oc % 3]]
                            pe_write(b)
                            first = True
                            if pair == 0:
                                T.matmul(b.t[:], BEB[:, oc * 128:(oc + 1) * 128],
                                         GTB[:, tg * TG:(tg + 1) * TG], start=True, stop=False)
                                first = False
                            for j, (e, s, zz) in enumerate(((e0, s0, zq0), (e1, s1, zq1))):
                                for dc in range(NDC):
                                    i = T.matmul(
                                        b.t[:],
                                        WS[:, s * NDC * D + dc * D + oc * 128:
                                              s * NDC * D + dc * D + oc * 128 + 128],
                                        ZE[:, zz * NDC * TG + dc * TG:zz * NDC * TG + dc * TG + TG],
                                        start=first, stop=(j == 1 and dc == NDC - 1))
                                    first = False
                                    if oc == NDC - 1 and dc == NDC - 1 and j == 0:
                                        z_free[zq0] = sT.inc(i)
                                        if tg == NTG - 1:
                                            slots_free[s0] = z_free[zq0]
                            pe_done(b, i)
                            if oc == NDC - 1:
                                z_free[zq1] = sT.n
                                if tg == NTG - 1:
                                    slots_free[s1] = sT.n
                            stop_n = sT.n
                            wv.wait(V, sT, stop_n)
                            dst = X[:, oc * CHUNK + tg * TG:oc * CHUNK + tg * TG + TG]
                            i = V.scalar_tensor_tensor(dst, b.t[:], 0.5, dst,
                                                       ALU.mult, ALU.add)
                            release(b, sV, i)
                            if pair == E // 2 - 1:
                                i = V.drain()
                                last_evac[(tg, oc)] = sV.inc(i)
                                if tail_hook is not None:
                                    tail_hook(tg, oc)
                            self.z_pump(stop_n)
                    if post_pair is not None:
                        post_pair(pair)
                i = V.drain()
                return sV.inc(i), last_evac

        def emit_attn(wkt_slot, zn_ready, hook=None, att_evac=None, k_hooks=None):
            if att_evac is None:
                att_evac = {}
            k_hooks = k_hooks or {}
            for tg in range(NTG):
                t0 = tg * TG
                kb_dones = []
                for kc in range(NDC):
                    b = banks[[0, 1, 6][kc % 3]]
                    pe_write(b)
                    wt.wait(T, sG, zn_ready[tg])
                    wt.wait(T, sWS[wkt_slot], slot_loaded[wkt_slot])
                    for hc in range(NDC):
                        i = T.matmul(b.t[:],
                                     WS[:, wkt_slot * NDC * D + hc * D + kc * 128:
                                           wkt_slot * NDC * D + hc * D + kc * 128 + 128],
                                     ZN[:, hc * CHUNK + t0:hc * CHUNK + t0 + TG],
                                     start=(hc == 0), stop=(hc == NDC - 1))
                    pe_done(b, i)
                    reader_wait(ws, S, b)
                    i = S.activation(SQ[:, (tg * NDC + kc) * TG:(tg * NDC + kc + 1) * TG],
                                     b.t[:], AF.Copy)
                    release(b, sS, i)
                    kb_dones.append(sS.n)
                    if tg == 0 and kc in k_hooks:
                        k_hooks[kc]()
                b = banks[4]
                pe_write(b)
                for kc in range(NDC):
                    wt.wait(T, sS, kb_dones[kc])
                    i = T.matmul(b.t[0:64, :], W1TB[:, kc * 64:(kc + 1) * 64],
                                 SQ[:, (tg * NDC + kc) * TG:(tg * NDC + kc + 1) * TG],
                                 start=(kc == 0), stop=(kc == NDC - 1))
                pe_done(b, i)
                reader_wait(ws, S, b)
                i = S.activation(ABUF[:], b.t[0:64, :], AF.Gelu)
                ab_done = sS.inc(i)
                release(b, sS, i, pre=ab_done)
                for hc in range(NDC):
                    b = banks[[0, 1, 6][hc % 3]]
                    pe_write(b)
                    wt.wait(T, sS, ab_done)
                    i = T.matmul(b.t[:], W2TB[:, hc * 128:(hc + 1) * 128], ABUF[:],
                                 start=True, stop=True)
                    pe_done(b, i)
                    reader_wait(wv, V, b)
                    i = V.tensor_tensor(X[:, hc * CHUNK + t0:hc * CHUNK + t0 + TG],
                                        X[:, hc * CHUNK + t0:hc * CHUNK + t0 + TG],
                                        b.t[:], ALU.add)
                    release(b, sV, i)
                    i = V.drain()
                    att_evac[(tg, hc)] = sV.inc(i)
                    if hook is not None:
                        hook(tg, hc)
            return sV.n, att_evac

        pending_ln1 = [None]

        def mk_ln_hook(ln, extra=None):
            def hook(tg, idx):
                if extra is not None:
                    extra(tg, idx)
                ln.cast(tg, idx)
                if tg == 1:
                    if idx == 0:
                        ln.stats_mean(0)
                    elif idx == 4:
                        ln.stats_s2(0)
                    elif idx == 5:
                        ln.rows(0)
                    elif idx == 6:
                        ln.bcast(0)
                    elif idx == 7:
                        ln.apply(0)
            return hook

        for c in range(n_chunk):
            if pending_ln1[0] is None:
                x_here = []
                for dc in range(NDC):
                    SY.dma_start(X[:, dc * CHUNK:(dc + 1) * CHUNK],
                                 x_r[:, dc, c * CHUNK:(c + 1) * CHUNK]).then_inc(sXd[dc].h, 16)
                    sXd[dc].n += 16
                    x_here.append(sXd[dc].n)
                xh = list(x_here)
                ln1 = LNPhase(lambda tg, dc: [(sXd[dc], xh[dc])])
            else:
                ln1 = pending_ln1[0]
            if c == 0:
                ws.wait(S, sW, init_w)
                wt.wait(T, sW, init_w)
                wv.wait(V, sW, init_w)
                wg.wait(G, sW, init_w)
            ln1.apply(0)

            my1 = {e: issue_weight_load("we", e) for e in range(4)}
            ep1 = ExpertPhase()
            ep1.zn_ready = ln1.zn_ready
            ep1.gtb_done = [0, 0]
            ln1.stats_mean(1)
            emit_prep(ln1.zn_ready,
                      hook=lambda tg: (ep1.gtb_done.__setitem__(tg, sV.n),
                                       ep1.prep_hook(tg)),
                      stage_hooks={"h0": lambda: ln1.stats_s2(1),
                                   "l0": lambda: ln1.rows(1),
                                   "c0": lambda: ln1.bcast(1),
                                   "t0": lambda: ln1.apply(1)})
            f2pre = {}

            def post_pair1(p):
                for e in (4 + 2 * p, 5 + 2 * p):
                    if e < E:
                        my1[e] = issue_weight_load("we", e)
                if p == 8:
                    f2pre["wkt"] = issue_weight_load("wkt")
                if p == 9:
                    f2pre[0] = issue_weight_load("we", 0)
                    f2pre[1] = issue_weight_load("we", 1)

            f1_evac = {}
            ln2 = LNPhase(lambda tg, dc: [(sV, f1_evac[(tg, dc)])])
            ep1.run(my1, post_pair1, tail_hook=mk_ln_hook(ln2), last_evac=f1_evac)
            ln2.stats_mean(1)
            att_evac = {}
            ln3 = LNPhase(lambda tg, dc: [(sV, att_evac[(tg, dc)])])
            emit_attn(f2pre["wkt"], ln2.zn_ready, hook=mk_ln_hook(ln3), att_evac=att_evac,
                      k_hooks={1: lambda: ln2.stats_s2(1),
                               3: lambda: ln2.rows(1),
                               5: lambda: ln2.bcast(1),
                               6: lambda: ln2.apply(1)})
            slots_free[f2pre["wkt"]] = sT.n
            f2pre[2] = issue_weight_load("we", 2)
            f2pre[3] = issue_weight_load("we", 3)
            ln3.apply(0)
            ln3.stats_mean(1)
            ep2 = ExpertPhase()
            ep2.zn_ready = ln3.zn_ready
            ep2.gtb_done = [0, 0]
            emit_prep(ln3.zn_ready,
                      hook=lambda tg: (ep2.gtb_done.__setitem__(tg, sV.n),
                                       ep2.prep_hook(tg)),
                      stage_hooks={"h0": lambda: ln3.stats_s2(1),
                                   "l0": lambda: ln3.rows(1),
                                   "c0": lambda: ln3.bcast(1),
                                   "t0": lambda: ln3.apply(1)})
            my2 = {e: f2pre[e] for e in range(4)}

            def post_pair2(p):
                for e in (4 + 2 * p, 5 + 2 * p):
                    if e < E:
                        my2[e] = issue_weight_load("we", e)

            f2_evac = {}
            nxt_here = {}

            def store_load(tg, dc):
                hsl = slice(dc * CHUNK + tg * TG, dc * CHUNK + (tg + 1) * TG)
                wy.wait(SY, sV, f2_evac[(tg, dc)])
                wy.wait(SY, sOd[dc], sOd[dc].n)
                wy.wait(SY, sXd[dc], sXd[dc].n)
                SY.dma_start(o_r[:, dc, c * CHUNK + tg * TG:c * CHUNK + (tg + 1) * TG],
                             X[:, hsl]).then_inc(sOd[dc].h, 16)
                sOd[dc].n += 16
                if c + 1 < n_chunk:
                    wy.wait(SY, sOd[dc], sOd[dc].n)
                    SY.dma_start(X[:, hsl],
                                 x_r[:, dc, (c + 1) * CHUNK + tg * TG:
                                            (c + 1) * CHUNK + (tg + 1) * TG]).then_inc(sXd[dc].h, 16)
                    sXd[dc].n += 16
                    nxt_here[(tg, dc)] = sXd[dc].n

            if c + 1 < n_chunk:
                lnext = LNPhase(lambda tg, dc: [(sXd[dc], nxt_here[(tg, dc)])])
                hook2 = mk_ln_hook(lnext, extra=store_load)
                pending_ln1[0] = lnext
            else:
                def hook2(tg, dc):
                    store_load(tg, dc)
                pending_ln1[0] = None
            ep2.run(my2, post_pair2, tail_hook=hook2, last_evac=f2_evac)
        for dc in range(NDC):
            SY.wait_ge(sOd[dc].h, 16 * n_chunk)
    return nc


_NC_CACHE = {}


def kernel(L0_attn_x, L0_attn_W_K, L0_attn_W_V, L0_attn_W1, L0_attn_W2,
           L0_attn_m1, L0_attn_m2, L0_ffn_wg1, L0_ffn_bg1, L0_ffn_wg2,
           L0_ffn_bg2, L0_ffn_We, L0_ffn_be):
    n_cores = 8
    x = np.asarray(L0_attn_x, dtype=np.float32)
    c32 = lambda a: np.ascontiguousarray(np.asarray(a, dtype=np.float32))

    feeds_common = {
        "wkt": c32(np.asarray(L0_attn_W_K).T),
        "w1t": c32(np.asarray(L0_attn_W1).T),
        "w2t": c32(np.asarray(L0_attn_W2).T),
        "wg1": c32(L0_ffn_wg1),
        "wg2": c32(L0_ffn_wg2),
        "bg1m": c32(np.asarray(L0_ffn_bg1).reshape(2, 128).T),
        "bg2r": c32(np.asarray(L0_ffn_bg2)[:E].reshape(1, E)),
        "be": c32(L0_ffn_be),
        "we": c32(np.asarray(L0_ffn_We).reshape(E * D, D)),
        "jm": np.full((128, 128), 1.0 / D, dtype=np.float32),
        "onescol": np.ones((128, 1), dtype=np.float32),
        "ones1": np.ones((1, 128), dtype=np.float32),
        "idn": np.eye(128, dtype=np.float32),
        "sel": np.repeat(np.eye(E, dtype=np.float32), 128, axis=1),
        "epsc": np.full((1, 1), EPS, dtype=np.float32),
    }
    in_maps = []
    for i in range(n_cores):
        f = dict(feeds_common)
        f["x_fm"] = c32(x[i].T)
        in_maps.append(f)

    key = TCORE // CHUNK
    if key not in _NC_CACHE:
        _NC_CACHE[key] = build(key)
    nc = _NC_CACHE[key]
    res = run_bass_kernel_spmd(nc, in_maps, core_ids=list(range(n_cores)))
    out = np.stack([np.asarray(r["out_fm"]).T for r in res.results], axis=0)
    return np.ascontiguousarray(out.astype(np.float32))


# version 14
# speedup vs baseline: 1.0221x; 1.0022x over previous
import numpy as np
import concourse.bass as bass
import concourse.mybir as mybir
from concourse.bass_utils import run_bass_kernel_spmd
from contextlib import ExitStack

F32 = mybir.dt.float32
BF16 = mybir.dt.bfloat16
AF = mybir.ActivationFunctionType
ALU = mybir.AluOpType
AX = mybir.AxisListType

D = 1024
NDC = 8
TCORE = 8192
CHUNK = 1024
TG = 512
NTG = CHUNK // TG
NTPT = TG // 128
E = 20
NSLOT = 4
H4 = 256
EPS = 1e-5


class Ctr:
    def __init__(self, h):
        self.h = h
        self.n = 0

    def inc(self, instr, k=1):
        instr.then_inc(self.h, k)
        self.n += k
        return self.n


class Waiter:
    def __init__(self):
        self.seen = {}

    def wait(self, eng, ctr, val):
        if val <= 0:
            return
        key = id(ctr)
        if self.seen.get(key, -1) >= val:
            return
        eng.wait_ge(ctr.h, val)
        self.seen[key] = val


class Bank:
    def __init__(self, t):
        self.t = t
        self.pe_done = 0
        self.released = None


def build(n_chunk):
    nc = bass.Bass("TRN2", target_bir_lowering=False)
    dt = nc.dram_tensor
    x_fm = dt("x_fm", [D, TCORE], F32, kind="ExternalInput").ap()
    wkt = dt("wkt", [D, D], F32, kind="ExternalInput").ap()
    w1t = dt("w1t", [D, 64], F32, kind="ExternalInput").ap()
    w2t = dt("w2t", [64, D], F32, kind="ExternalInput").ap()
    wg1 = dt("wg1", [D, H4], F32, kind="ExternalInput").ap()
    wg2 = dt("wg2", [H4, 40], F32, kind="ExternalInput").ap()
    bg1m = dt("bg1m", [128, 2], F32, kind="ExternalInput").ap()
    bg2r = dt("bg2r", [1, E], F32, kind="ExternalInput").ap()
    be = dt("be", [E, D], F32, kind="ExternalInput").ap()
    we = dt("we", [E * D, D], F32, kind="ExternalInput").ap()
    jm = dt("jm", [128, 128], F32, kind="ExternalInput").ap()
    onescol = dt("onescol", [128, 1], F32, kind="ExternalInput").ap()
    ones1 = dt("ones1", [1, 128], F32, kind="ExternalInput").ap()
    idn = dt("idn", [128, 128], F32, kind="ExternalInput").ap()
    sel = dt("sel", [E, E * 128], F32, kind="ExternalInput").ap()
    epsc = dt("epsc", [1, 1], F32, kind="ExternalInput").ap()
    out_fm = dt("out_fm", [D, TCORE], F32, kind="ExternalOutput").ap()

    we_r = we.rearrange("(e dc p) o -> e p dc o", e=E, dc=NDC, p=128)
    x_r = x_fm.rearrange("(dc p) t -> p dc t", p=128)
    o_r = out_fm.rearrange("(dc p) t -> p dc t", p=128)

    with ExitStack() as es:
        sb = lambda name, shape, dtype: es.enter_context(nc.sbuf_tensor(name, shape, dtype))
        ps = lambda name, shape, dtype: es.enter_context(nc.psum_tensor(name, shape, dtype))

        X = sb("X", [128, NDC * CHUNK], F32)
        ZN = sb("ZN", [128, NDC * CHUNK], BF16)
        SQ = sb("SQ", [128, 2 * NDC * TG], BF16)
        WS = sb("WS", [128, NSLOT * NDC * D], BF16)
        ZE = sb("ZE", [128, 4 * NDC * TG], BF16)
        ROWA = sb("ROWA", [1, 2 * TG], F32)
        ROWB = sb("ROWB", [1, 2 * TG], F32)
        HB = sb("HB", [128, 2 * TG], BF16)
        ABUF = sb("ABUF", [64, TG], BF16)
        GBB = sb("GBB", [128, 3 * CHUNK], BF16)
        MBS = sb("MBS", [128, 2 * TG], BF16)
        RBS = sb("RBS", [128, 2 * TG], BF16)
        GTB = sb("GTB", [E, CHUNK], BF16)
        GBT = sb("GBT", [128, NTG * NTPT * E], BF16)
        PRB = sb("PRB", [128, NTPT * E], F32)
        MSK = sb("MSK", [128, NTPT * E], F32)
        TM2 = sb("TM2", [128, NTPT * E], F32)
        NRM = sb("NRM", [128, NTPT], F32)
        SUMS = sb("SUMS", [128, NTPT], F32)
        RS = sb("RS", [128, NTPT], F32)
        M1 = sb("M1", [128, NTPT], F32)
        W1TB = sb("W1TB", [128, NDC * 64], BF16)
        W2TB = sb("W2TB", [64, D], BF16)
        WG1B = sb("WG1B", [128, NDC * H4], BF16)
        WG2B = sb("WG2B", [128, 2 * E], BF16)
        BEB = sb("BEB", [E, D], BF16)
        BG1C = sb("BG1C", [128, 2], F32)
        BG2R = sb("BG2R", [1, E], BF16)
        JB = sb("JB", [128, 128], BF16)
        OCB = sb("OCB", [128, 1], BF16)
        O1B = sb("O1B", [1, 128], BF16)
        IDNB = sb("IDNB", [128, 128], BF16)
        SELB = sb("SELB", [E, E * 128], BF16)
        EPSB = sb("EPSB", [1, 1], F32)

        banks = [Bank(ps(f"pb{i}", [128, 512], F32)) for i in range(7)]
        pGT = ps("pGT", [E, 128], BF16)
        pgt_pe_done = [0]
        pgt_released = [None]

        sT = Ctr(es.enter_context(nc.semaphore("sT")))
        sV = Ctr(es.enter_context(nc.semaphore("sV")))
        sS = Ctr(es.enter_context(nc.semaphore("sS")))
        sG = Ctr(es.enter_context(nc.semaphore("sG")))
        sW = Ctr(es.enter_context(nc.semaphore("sW")))
        sWS = [Ctr(es.enter_context(nc.semaphore(f"sWS{i}"))) for i in range(NSLOT)]
        sXd = [Ctr(es.enter_context(nc.semaphore(f"sXd{i}"))) for i in range(NDC)]
        sOd = [Ctr(es.enter_context(nc.semaphore(f"sOd{i}"))) for i in range(NDC)]

        blk = es.enter_context(nc.Block())
        T, V, S, G, SY = nc.tensor, nc.vector, nc.scalar, nc.gpsimd, nc.sync
        wt, wv, ws, wg, wy = Waiter(), Waiter(), Waiter(), Waiter(), Waiter()

        def pe_write(bank):
            if bank.released is not None:
                wt.wait(T, bank.released[0], bank.released[1])

        def pe_done(bank, instr):
            bank.pe_done = sT.inc(instr)

        def reader_wait(w, eng, bank):
            w.wait(eng, sT, bank.pe_done)

        def release(bank, ctr, instr, pre=None):
            n = pre if pre is not None else ctr.inc(instr)
            bank.released = (ctr, n)
            return n

        G.dma_start(W1TB[:].rearrange("p (kc d) -> p kc d", kc=NDC),
                    w1t.rearrange("(kc p) d -> p kc d", p=128)).then_inc(sW.h, 16)
        G.dma_start(W2TB[:], w2t).then_inc(sW.h, 16)
        G.dma_start(WG1B[:].rearrange("p (dc h) -> p dc h", dc=NDC),
                    wg1.rearrange("(dc p) h -> p dc h", p=128)).then_inc(sW.h, 16)
        G.dma_start(WG2B[:].rearrange("p (hc e) -> p hc e", hc=2),
                    wg2.rearrange("(hc p) e -> p hc e", p=128)[:, :, 0:E]).then_inc(sW.h, 16)
        G.dma_start(BEB[:], be).then_inc(sW.h, 16)
        G.dma_start(BG2R[:], bg2r).then_inc(sW.h, 16)
        G.dma_start(JB[:], jm).then_inc(sW.h, 16)
        G.dma_start(OCB[:], onescol).then_inc(sW.h, 16)
        G.dma_start(O1B[:], ones1).then_inc(sW.h, 16)
        G.dma_start(IDNB[:], idn).then_inc(sW.h, 16)
        G.dma_start(SELB[:], sel).then_inc(sW.h, 16)
        G.dma_start(BG1C[:], bg1m).then_inc(sW.h, 16)
        G.dma_start(EPSB[:], epsc).then_inc(sW.h, 16)
        sW.n += 16 * 13
        init_w = sW.n

        slots_free = [0] * NSLOT
        slot_loaded = [0] * NSLOT
        slot_rr = [0]

        def issue_weight_load(kind, e=None):
            s = slot_rr[0] % NSLOT
            slot_rr[0] += 1
            wg.wait(G, sT, slots_free[s])
            dst = WS[:, s * NDC * D:(s + 1) * NDC * D].rearrange("p (dc o) -> p dc o", dc=NDC)
            if kind == "we":
                G.dma_start(dst, we_r[e]).then_inc(sWS[s].h, 16)
            else:
                G.dma_start(dst, wkt.rearrange("(hc p) k -> p hc k", p=128)).then_inc(sWS[s].h, 16)
            sWS[s].n += 16
            slot_loaded[s] = sWS[s].n
            return s

        ln_pool_prev = [0, 0]
        rbs_guard = [[], []]

        class LNPhase:

            def __init__(self, x_dep):
                self.x_dep = x_dep
                self.tgb = {0: (banks[2], banks[3]), 1: (banks[4], banks[5])}
                self.cast_a = {}
                self.cast_v = {}
                self.subs_done = {}
                self.sq_done = {}
                self.rstd_done = {}
                self.zn_ready = [0, 0]
                self.done_cast = set()
                self.done = set()

            def cast(self, tg, dc):
                if (tg, dc) in self.done_cast:
                    return
                self.done_cast.add((tg, dc))
                t0 = tg * TG
                sl = slice((tg * NDC + dc) * TG, (tg * NDC + dc + 1) * TG)
                xs = X[:, dc * CHUNK + t0:dc * CHUNK + t0 + TG]
                if dc % 2 == 0:
                    for (ctr, cnt) in self.x_dep(tg, dc):
                        ws.wait(S, ctr, cnt)
                    i = S.activation(SQ[:, sl], xs, AF.Copy)
                    self.cast_a[tg] = sS.inc(i)
                else:
                    for (ctr, cnt) in self.x_dep(tg, dc):
                        wv.wait(V, ctr, cnt)
                    i = V.tensor_copy(SQ[:, sl], xs)
                    if dc == NDC - 1:
                        i = V.drain()
                    self.cast_v[tg] = sV.inc(i)

            def casts(self, tg):
                for dc in range(NDC):
                    self.cast(tg, dc)

            def stats_mean(self, tg):
                if ("stats_mean", tg) in self.done:
                    return
                self.done.add(("stats_mean", tg))
                self.casts(tg)
                bM, bR = self.tgb[tg]
                pe_write(bM)
                wt.wait(T, sS, self.cast_a[tg])
                wt.wait(T, sV, self.cast_v[tg])
                for dc in range(NDC):
                    i = T.matmul(bM.t[:], JB[:],
                                 SQ[:, (tg * NDC + dc) * TG:(tg * NDC + dc + 1) * TG],
                                 start=(dc == 0), stop=(dc == NDC - 1))
                pe_done(bM, i)
                reader_wait(ws, S, bM)
                ws.wait(S, sG, ln_pool_prev[tg])
                for (ctr, cnt) in rbs_guard[tg]:
                    ws.wait(S, ctr, cnt)
                i = S.activation(MBS[:, tg * TG:(tg + 1) * TG], bM.t[:], AF.Copy)
                mbs_done = sS.inc(i)
                bM.released = (sS, mbs_done)
                t0 = tg * TG
                rsl = slice(tg * TG, (tg + 1) * TG)
                wv.wait(V, sS, mbs_done)
                zn3 = bass.AP(ZN[:].tensor, t0,
                              [[NDC * CHUNK, 128], [CHUNK, NDC], [1, TG]])
                sq3 = bass.AP(SQ[:].tensor, tg * NDC * TG,
                              [[2 * NDC * TG, 128], [TG, NDC], [1, TG]])
                mb3 = bass.AP(MBS[:].tensor, tg * TG,
                              [[2 * TG, 128], [0, NDC], [1, TG]])
                V.tensor_tensor(zn3, sq3, mb3, ALU.subtract)
                i = V.drain()
                self.subs_done[tg] = sV.inc(i)

            def stats_s2(self, tg):
                if ("stats_s2", tg) in self.done:
                    return
                self.done.add(("stats_s2", tg))
                self.stats_mean(tg)
                bR = self.tgb[tg][1]
                ssl = slice(tg * NDC * TG, (tg + 1) * NDC * TG)
                i = V.tensor_tensor(SQ[:, ssl], SQ[:, ssl], SQ[:, ssl], ALU.mult)
                i = V.drain()
                self.sq_done[tg] = sV.inc(i)
                pe_write(bR)
                wt.wait(T, sV, self.sq_done[tg])
                for dc in range(NDC):
                    i = T.matmul(bR.t[0:1, :], OCB[:],
                                 SQ[:, (tg * NDC + dc) * TG:(tg * NDC + dc + 1) * TG],
                                 start=(dc == 0), stop=(dc == NDC - 1))
                pe_done(bR, i)

            def rows(self, tg):
                if ("rows", tg) in self.done:
                    return
                self.done.add(("rows", tg))
                self.stats_s2(tg)
                bR = self.tgb[tg][1]
                rsl = slice(tg * TG, (tg + 1) * TG)
                reader_wait(ws, S, bR)
                S.drain()
                S.activation(ROWA[0:1, rsl], bR.t[0:1, :], AF.Copy, scale=1.0 / D)
                i = S.activation(ROWB[0:1, rsl], MBS[0:1, rsl], AF.Square)
                rows_done = sS.inc(i)
                wv.wait(V, sS, rows_done)
                V.tensor_tensor(ROWA[0:1, rsl], ROWA[0:1, rsl], ROWB[0:1, rsl], ALU.subtract)
                i = V.drain()
                var_done = sV.inc(i)
                ws.wait(S, sV, var_done)
                S.activation(ROWB[0:1, rsl], ROWA[0:1, rsl], AF.Ln, bias=EPSB[0:1, :])
                S.drain()
                i = S.activation(RBS[0:1, rsl], ROWB[0:1, rsl], AF.Exp, scale=-0.5)
                self.rstd_done[tg] = sS.inc(i)

            def bcast(self, tg):
                if ("bcast", tg) in self.done:
                    return
                self.done.add(("bcast", tg))
                self.rows(tg)
                bR = self.tgb[tg][1]
                rsl = slice(tg * TG, (tg + 1) * TG)
                wt.wait(T, sS, self.rstd_done[tg])
                i = T.matmul(bR.t[:], O1B[:], RBS[0:1, rsl], start=True, stop=True)
                pe_done(bR, i)
                reader_wait(ws, S, bR)
                i = S.activation(RBS[:, rsl], bR.t[:], AF.Copy)
                self.rbs_done = sS.inc(i)
                release(bR, sS, i, pre=self.rbs_done)

            def apply(self, tg):
                if ("apply", tg) in self.done:
                    return
                self.done.add(("apply", tg))
                self.bcast(tg)
                rsl = slice(tg * TG, (tg + 1) * TG)
                t0 = tg * TG
                wv.wait(V, sS, self.rbs_done)
                zn3 = bass.AP(ZN[:].tensor, t0,
                              [[NDC * CHUNK, 128], [CHUNK, NDC], [1, TG]])
                rb3 = bass.AP(RBS[:].tensor, tg * TG,
                              [[2 * TG, 128], [0, NDC], [1, TG]])
                V.tensor_tensor(zn3, zn3, rb3, ALU.mult)
                i = V.drain()
                dve_mul_done = sV.inc(i)
                wg.wait(G, sV, dve_mul_done)
                i = G.engine_nop()
                self.zn_ready[tg] = sG.inc(i)
                ln_pool_prev[tg] = self.zn_ready[tg]
                rbs_guard[tg] = []

            def finish(self):
                for tg in range(NTG):
                    self.apply(tg)
                return self.zn_ready

        def emit_prep(zn_ready, hook=None, stage_hooks=None):
            chains = {}

            def tg_block(tg, shooks):
                t0 = tg * TG
                hb_banks = []
                for hc in range(2):
                    b = banks[hc]
                    pe_write(b)
                    wt.wait(T, sG, zn_ready[tg])
                    for dc in range(NDC):
                        i = T.matmul(b.t[:], WG1B[:, dc * H4 + hc * 128:dc * H4 + hc * 128 + 128],
                                     ZN[:, dc * CHUNK + t0:dc * CHUNK + t0 + TG],
                                     start=(dc == 0), stop=(dc == NDC - 1))
                    pe_done(b, i)
                    hb_banks.append(b)
                for hc in range(2):
                    b = hb_banks[hc]
                    reader_wait(ws, S, b)
                    i = S.activation(HB[:, hc * TG:(hc + 1) * TG], b.t[:], AF.Gelu,
                                     bias=BG1C[:, hc:hc + 1])
                    release(b, sS, i)
                if shooks and "h0" in shooks:
                    shooks["h0"]()
                hb_done = sS.n
                lb = banks[2 + tg]
                pe_write(lb)
                wt.wait(T, sS, hb_done)
                for ts in range(NTPT):
                    T.matmul(lb.t[:, ts * 32:ts * 32 + E],
                             HB[:, 0 * TG + ts * 128:0 * TG + ts * 128 + 128],
                             WG2B[:, 0:E], start=True, stop=False)
                    T.matmul(lb.t[:, ts * 32:ts * 32 + E],
                             HB[:, 1 * TG + ts * 128:1 * TG + ts * 128 + 128],
                             WG2B[:, E:2 * E], start=False, stop=False)
                    i = T.matmul(lb.t[:, ts * 32:ts * 32 + E], O1B[:], BG2R[:],
                                 start=False, stop=True)
                pe_done(lb, i)
                if shooks and "l0" in shooks:
                    shooks["l0"]()
                lb3 = bass.AP(lb.t[:].tensor, 0, [[512, 128], [32, NTPT], [1, E]])
                reader_wait(wv, V, lb)
                V.tensor_reduce(NRM[:], lb3, axis=AX.X, op=ALU.max, negate=True)
                i = V.drain()
                nrm_done = sV.inc(i)
                for ts in range(NTPT):
                    ws.wait(S, sV, nrm_done)
                    i = S.activation(PRB[:, ts * E:(ts + 1) * E], lb.t[:, ts * 32:ts * 32 + E],
                                     AF.Exp, bias=NRM[:, ts:ts + 1], accum_out=SUMS[:, ts:ts + 1])
                ed = sS.inc(i)
                release(lb, sS, i, pre=ed)
                prb3 = PRB[:].rearrange("p (s e) -> p s e", e=E)
                msk3 = MSK[:].rearrange("p (s e) -> p s e", e=E)
                tm23 = TM2[:].rearrange("p (s e) -> p s e", e=E)
                m2b = bass.AP(M1[:].tensor, 0, [[NTPT, 128], [1, NTPT], [0, E]])
                rsb = bass.AP(RS[:].tensor, 0, [[NTPT, 128], [1, NTPT], [0, E]])
                wv.wait(V, sS, ed)
                V.tensor_scalar(MSK[:], PRB[:], 1.0, None, ALU.is_ge)
                V.reciprocal(RS[:], SUMS[:])
                V.drain()
                V.scalar_tensor_tensor(MSK[:], MSK[:], -1e9, PRB[:], ALU.mult, ALU.add)
                V.drain()
                V.tensor_reduce(M1[:], msk3, axis=AX.X, op=ALU.max)
                V.drain()
                V.tensor_tensor(msk3, prb3, m2b, ALU.is_ge)
                V.tensor_tensor(tm23, prb3, rsb, ALU.mult)
                V.drain()
                V.tensor_tensor(GBT[:, tg * NTPT * E:(tg + 1) * NTPT * E], TM2[:], MSK[:],
                                ALU.mult)
                i = V.drain()
                chains[tg] = sV.inc(i)

            def transposes(tg):
                for ts in range(NTPT):
                    if pgt_released[0] is not None:
                        wt.wait(T, pgt_released[0][0], pgt_released[0][1])
                    wt.wait(T, sV, chains[tg])
                    i = T.transpose(pGT[:], GBT[:, (tg * NTPT + ts) * E:(tg * NTPT + ts + 1) * E],
                                    IDNB[:])
                    pgt_pe_done[0] = sT.inc(i)
                    wv.wait(V, sT, pgt_pe_done[0])
                    i = V.tensor_copy(GTB[:, (tg * NTPT + ts) * 128:(tg * NTPT + ts) * 128 + 128],
                                      pGT[:])
                    sV.inc(i)
                    pgt_released[0] = (sV, sV.n)
                i = V.drain()
                sV.inc(i)
                if hook is not None:
                    hook(tg)

            tg_block(0, stage_hooks)
            if stage_hooks and "c0" in stage_hooks:
                stage_hooks["c0"]()
            if stage_hooks and "t0" in stage_hooks:
                stage_hooks["t0"]()
            tg_block(1, None)
            transposes(0)
            transposes(1)

        z_free = [0] * 4
        gbb_free = {}

        class ExpertPhase:

            def __init__(self):
                self.gb_banks = [banks[2], banks[3], banks[4], banks[5]]
                self.zorder = []
                for pair in range(E // 2):
                    for tg in range(NTG):
                        for e in (2 * pair, 2 * pair + 1):
                            self.zorder.append((e, tg))
                self.z_done = {}
                self.z_sl = {}
                self.gbb_req = {}
                self.zq = 0
                self.gb_emitted = set()

            def gb_half(self, pair, tg):
                for j in range(2):
                    e = 2 * pair + j
                    b = self.gb_banks[2 * j + tg]
                    pe_write(b)
                    wt.wait(T, sV, self.gtb_done[tg])
                    i = T.matmul(b.t[:], SELB[:, e * 128:(e + 1) * 128],
                                 GTB[:, tg * TG:(tg + 1) * TG], start=True, stop=True)
                    pe_done(b, i)
                for j in range(2):
                    e = 2 * pair + j
                    zs = e % 3
                    b = self.gb_banks[2 * j + tg]
                    ws.wait(S, sV, gbb_free.get((zs, tg), 0))
                    reader_wait(ws, S, b)
                    i = S.activation(GBB[:, zs * CHUNK + tg * TG:zs * CHUNK + (tg + 1) * TG],
                                     b.t[:], AF.Copy)
                    release(b, sS, i)
                    self.gbb_req[(e, tg)] = sS.n
                self.gb_emitted.add((pair, tg))

            def z_pump(self, pe_reachable):
                while self.zq < len(self.zorder):
                    e, tg = self.zorder[self.zq]
                    if (e, tg) not in self.gbb_req:
                        break
                    slot = self.zq % 4
                    if z_free[slot] is None or z_free[slot] > pe_reachable:
                        break
                    prev_free = z_free[slot]
                    self.zq += 1
                    self.z_sl[(e, tg)] = slot
                    z_free[slot] = None
                    wv.wait(V, sS, self.gbb_req[(e, tg)])
                    wv.wait(V, sT, prev_free)
                    wv.wait(V, sG, self.zn_ready[tg])
                    zs = e % 3
                    ze3 = bass.AP(ZE[:].tensor, slot * NDC * TG,
                                  [[4 * NDC * TG, 128], [TG, NDC], [1, TG]])
                    zn3 = bass.AP(ZN[:].tensor, tg * TG,
                                  [[NDC * CHUNK, 128], [CHUNK, NDC], [1, TG]])
                    gb3 = bass.AP(GBB[:].tensor, zs * CHUNK + tg * TG,
                                  [[3 * CHUNK, 128], [0, NDC], [1, TG]])
                    V.tensor_tensor(ze3, zn3, gb3, ALU.mult)
                    i = V.drain()
                    n = sV.inc(i)
                    self.z_done[(e, tg)] = n
                    gbb_free[(zs, tg)] = n

            def prep_hook(self, tg):
                self.gb_half(0, tg)
                self.z_pump(sT.n)

            def run(self, eslots, post_pair=None, tail_hook=None, last_evac=None):
                if last_evac is None:
                    last_evac = {}
                for pair in range(E // 2):
                    e0, e1 = 2 * pair, 2 * pair + 1
                    s0, s1 = eslots[e0], eslots[e1]
                    if pair + 1 < E // 2:
                        for tg in range(NTG):
                            self.gb_half(pair + 1, tg)
                    wt.wait(T, sWS[s0], slot_loaded[s0])
                    wt.wait(T, sWS[s1], slot_loaded[s1])
                    for tg in range(NTG):
                        wt.wait(T, sV, self.z_done[(e0, tg)])
                        wt.wait(T, sV, self.z_done[(e1, tg)])
                        zq0, zq1 = self.z_sl[(e0, tg)], self.z_sl[(e1, tg)]
                        for oc in range(NDC):
                            b = banks[[0, 1, 6][oc % 3]]
                            pe_write(b)
                            first = True
                            if pair == 0:
                                T.matmul(b.t[:], BEB[:, oc * 128:(oc + 1) * 128],
                                         GTB[:, tg * TG:(tg + 1) * TG], start=True, stop=False)
                                first = False
                            for j, (e, s, zz) in enumerate(((e0, s0, zq0), (e1, s1, zq1))):
                                for dc in range(NDC):
                                    i = T.matmul(
                                        b.t[:],
                                        WS[:, s * NDC * D + dc * D + oc * 128:
                                              s * NDC * D + dc * D + oc * 128 + 128],
                                        ZE[:, zz * NDC * TG + dc * TG:zz * NDC * TG + dc * TG + TG],
                                        start=first, stop=(j == 1 and dc == NDC - 1))
                                    first = False
                                    if oc == NDC - 1 and dc == NDC - 1 and j == 0:
                                        z_free[zq0] = sT.inc(i)
                                        if tg == NTG - 1:
                                            slots_free[s0] = z_free[zq0]
                            pe_done(b, i)
                            if oc == NDC - 1:
                                z_free[zq1] = sT.n
                                if tg == NTG - 1:
                                    slots_free[s1] = sT.n
                            stop_n = sT.n
                            wv.wait(V, sT, stop_n)
                            dst = X[:, oc * CHUNK + tg * TG:oc * CHUNK + tg * TG + TG]
                            i = V.scalar_tensor_tensor(dst, b.t[:], 0.5, dst,
                                                       ALU.mult, ALU.add)
                            release(b, sV, i)
                            if pair == E // 2 - 1:
                                i = V.drain()
                                last_evac[(tg, oc)] = sV.inc(i)
                                if tail_hook is not None:
                                    tail_hook(tg, oc)
                            self.z_pump(stop_n)
                    if post_pair is not None:
                        post_pair(pair)
                i = V.drain()
                return sV.inc(i), last_evac

        def emit_attn(wkt_slot, zn_ready, hook=None, att_evac=None, k_hooks=None):
            if att_evac is None:
                att_evac = {}
            k_hooks = k_hooks or {}
            for tg in range(NTG):
                t0 = tg * TG
                kb_dones = []
                for kc in range(NDC):
                    b = banks[[0, 1, 6][kc % 3]]
                    pe_write(b)
                    wt.wait(T, sG, zn_ready[tg])
                    wt.wait(T, sWS[wkt_slot], slot_loaded[wkt_slot])
                    for hc in range(NDC):
                        i = T.matmul(b.t[:],
                                     WS[:, wkt_slot * NDC * D + hc * D + kc * 128:
                                           wkt_slot * NDC * D + hc * D + kc * 128 + 128],
                                     ZN[:, hc * CHUNK + t0:hc * CHUNK + t0 + TG],
                                     start=(hc == 0), stop=(hc == NDC - 1))
                    pe_done(b, i)
                    reader_wait(ws, S, b)
                    i = S.activation(SQ[:, (tg * NDC + kc) * TG:(tg * NDC + kc + 1) * TG],
                                     b.t[:], AF.Copy)
                    release(b, sS, i)
                    kb_dones.append(sS.n)
                    if tg == 0 and kc in k_hooks:
                        k_hooks[kc]()
                b = banks[4]
                pe_write(b)
                for kc in range(NDC):
                    wt.wait(T, sS, kb_dones[kc])
                    i = T.matmul(b.t[0:64, :], W1TB[:, kc * 64:(kc + 1) * 64],
                                 SQ[:, (tg * NDC + kc) * TG:(tg * NDC + kc + 1) * TG],
                                 start=(kc == 0), stop=(kc == NDC - 1))
                pe_done(b, i)
                reader_wait(ws, S, b)
                i = S.activation(ABUF[:], b.t[0:64, :], AF.Gelu)
                ab_done = sS.inc(i)
                release(b, sS, i, pre=ab_done)
                for hc in range(NDC):
                    b = banks[[0, 1, 6][hc % 3]]
                    pe_write(b)
                    wt.wait(T, sS, ab_done)
                    i = T.matmul(b.t[:], W2TB[:, hc * 128:(hc + 1) * 128], ABUF[:],
                                 start=True, stop=True)
                    pe_done(b, i)
                    reader_wait(wv, V, b)
                    i = V.tensor_tensor(X[:, hc * CHUNK + t0:hc * CHUNK + t0 + TG],
                                        X[:, hc * CHUNK + t0:hc * CHUNK + t0 + TG],
                                        b.t[:], ALU.add)
                    release(b, sV, i)
                    i = V.drain()
                    att_evac[(tg, hc)] = sV.inc(i)
                    if hook is not None:
                        hook(tg, hc)
            return sV.n, att_evac

        pending_ln1 = [None]

        def mk_ln_hook(ln, extra=None):
            def hook(tg, idx):
                if extra is not None:
                    extra(tg, idx)
                ln.cast(tg, idx)
                if tg == 1:
                    if idx == 0:
                        ln.stats_mean(0)
                    elif idx == 4:
                        ln.stats_s2(0)
                    elif idx == 5:
                        ln.rows(0)
                    elif idx == 6:
                        ln.bcast(0)
                    elif idx == 7:
                        ln.apply(0)
            return hook

        for c in range(n_chunk):
            if pending_ln1[0] is None:
                x_here = []
                for dc in range(NDC):
                    SY.dma_start(X[:, dc * CHUNK:(dc + 1) * CHUNK],
                                 x_r[:, dc, c * CHUNK:(c + 1) * CHUNK]).then_inc(sXd[dc].h, 16)
                    sXd[dc].n += 16
                    x_here.append(sXd[dc].n)
                xh = list(x_here)
                ln1 = LNPhase(lambda tg, dc: [(sXd[dc], xh[dc])])
            else:
                ln1 = pending_ln1[0]
            if c == 0:
                ws.wait(S, sW, init_w)
                wt.wait(T, sW, init_w)
                wv.wait(V, sW, init_w)
                wg.wait(G, sW, init_w)
            ln1.apply(0)

            my1 = {e: issue_weight_load("we", e) for e in range(4)}
            ep1 = ExpertPhase()
            ep1.zn_ready = ln1.zn_ready
            ep1.gtb_done = [0, 0]
            ln1.stats_mean(1)
            emit_prep(ln1.zn_ready,
                      hook=lambda tg: (ep1.gtb_done.__setitem__(tg, sV.n),
                                       ep1.prep_hook(tg)),
                      stage_hooks={"h0": lambda: ln1.stats_s2(1),
                                   "l0": lambda: ln1.rows(1),
                                   "c0": lambda: ln1.bcast(1),
                                   "t0": lambda: ln1.apply(1)})
            f2pre = {}

            def post_pair1(p):
                for e in (4 + 2 * p, 5 + 2 * p):
                    if e < E:
                        my1[e] = issue_weight_load("we", e)
                if p == 8:
                    f2pre["wkt"] = issue_weight_load("wkt")
                if p == 9:
                    f2pre[0] = issue_weight_load("we", 0)
                    f2pre[1] = issue_weight_load("we", 1)

            f1_evac = {}
            ln2 = LNPhase(lambda tg, dc: [(sV, f1_evac[(tg, dc)])])
            ep1.run(my1, post_pair1, tail_hook=mk_ln_hook(ln2), last_evac=f1_evac)
            ln2.stats_mean(1)
            att_evac = {}
            ln3 = LNPhase(lambda tg, dc: [(sV, att_evac[(tg, dc)])])
            emit_attn(f2pre["wkt"], ln2.zn_ready, hook=mk_ln_hook(ln3), att_evac=att_evac,
                      k_hooks={1: lambda: ln2.stats_s2(1),
                               3: lambda: ln2.rows(1),
                               5: lambda: ln2.bcast(1),
                               6: lambda: ln2.apply(1)})
            slots_free[f2pre["wkt"]] = sT.n
            f2pre[2] = issue_weight_load("we", 2)
            f2pre[3] = issue_weight_load("we", 3)
            ln3.apply(0)
            ln3.stats_mean(1)
            ep2 = ExpertPhase()
            ep2.zn_ready = ln3.zn_ready
            ep2.gtb_done = [0, 0]
            emit_prep(ln3.zn_ready,
                      hook=lambda tg: (ep2.gtb_done.__setitem__(tg, sV.n),
                                       ep2.prep_hook(tg)),
                      stage_hooks={"h0": lambda: ln3.stats_s2(1),
                                   "l0": lambda: ln3.rows(1),
                                   "c0": lambda: ln3.bcast(1),
                                   "t0": lambda: ln3.apply(1)})
            my2 = {e: f2pre[e] for e in range(4)}

            def post_pair2(p):
                for e in (4 + 2 * p, 5 + 2 * p):
                    if e < E:
                        my2[e] = issue_weight_load("we", e)

            f2_evac = {}
            nxt_here = {}

            def store_load(tg, dc):
                hsl = slice(dc * CHUNK + tg * TG, dc * CHUNK + (tg + 1) * TG)
                wy.wait(SY, sV, f2_evac[(tg, dc)])
                wy.wait(SY, sOd[dc], sOd[dc].n)
                wy.wait(SY, sXd[dc], sXd[dc].n)
                SY.dma_start(o_r[:, dc, c * CHUNK + tg * TG:c * CHUNK + (tg + 1) * TG],
                             X[:, hsl]).then_inc(sOd[dc].h, 16)
                sOd[dc].n += 16
                if c + 1 < n_chunk:
                    wy.wait(SY, sOd[dc], sOd[dc].n)
                    SY.dma_start(X[:, hsl],
                                 x_r[:, dc, (c + 1) * CHUNK + tg * TG:
                                            (c + 1) * CHUNK + (tg + 1) * TG]).then_inc(sXd[dc].h, 16)
                    sXd[dc].n += 16
                    nxt_here[(tg, dc)] = sXd[dc].n

            if c + 1 < n_chunk:
                lnext = LNPhase(lambda tg, dc: [(sXd[dc], nxt_here[(tg, dc)])])
                hook2 = mk_ln_hook(lnext, extra=store_load)
                pending_ln1[0] = lnext
            else:
                def hook2(tg, dc):
                    store_load(tg, dc)
                pending_ln1[0] = None
            ep2.run(my2, post_pair2, tail_hook=hook2, last_evac=f2_evac)
        for dc in range(NDC):
            SY.wait_ge(sOd[dc].h, 16 * n_chunk)
    return nc


_NC_CACHE = {}


def kernel(L0_attn_x, L0_attn_W_K, L0_attn_W_V, L0_attn_W1, L0_attn_W2,
           L0_attn_m1, L0_attn_m2, L0_ffn_wg1, L0_ffn_bg1, L0_ffn_wg2,
           L0_ffn_bg2, L0_ffn_We, L0_ffn_be):
    n_cores = 8
    x = np.asarray(L0_attn_x, dtype=np.float32)
    c32 = lambda a: np.ascontiguousarray(np.asarray(a, dtype=np.float32))

    feeds_common = {
        "wkt": c32(np.asarray(L0_attn_W_K).T),
        "w1t": c32(np.asarray(L0_attn_W1).T),
        "w2t": c32(np.asarray(L0_attn_W2).T),
        "wg1": c32(L0_ffn_wg1),
        "wg2": c32(L0_ffn_wg2),
        "bg1m": c32(np.asarray(L0_ffn_bg1).reshape(2, 128).T),
        "bg2r": c32(np.asarray(L0_ffn_bg2)[:E].reshape(1, E)),
        "be": c32(L0_ffn_be),
        "we": c32(np.asarray(L0_ffn_We).reshape(E * D, D)),
        "jm": np.full((128, 128), 1.0 / D, dtype=np.float32),
        "onescol": np.ones((128, 1), dtype=np.float32),
        "ones1": np.ones((1, 128), dtype=np.float32),
        "idn": np.eye(128, dtype=np.float32),
        "sel": np.repeat(np.eye(E, dtype=np.float32), 128, axis=1),
        "epsc": np.full((1, 1), EPS, dtype=np.float32),
    }
    in_maps = []
    for i in range(n_cores):
        f = dict(feeds_common)
        f["x_fm"] = c32(x[i].T)
        in_maps.append(f)

    key = TCORE // CHUNK
    if key not in _NC_CACHE:
        _NC_CACHE[key] = build(key)
    nc = _NC_CACHE[key]
    res = run_bass_kernel_spmd(nc, in_maps, core_ids=list(range(n_cores)))
    out = np.stack([np.asarray(r["out_fm"]).T for r in res.results], axis=0)
    return np.ascontiguousarray(out.astype(np.float32))


# version 15
# speedup vs baseline: 1.0234x; 1.0013x over previous
import numpy as np
import concourse.bass as bass
import concourse.mybir as mybir
from concourse.bass_utils import run_bass_kernel_spmd
from contextlib import ExitStack

F32 = mybir.dt.float32
BF16 = mybir.dt.bfloat16
AF = mybir.ActivationFunctionType
ALU = mybir.AluOpType
AX = mybir.AxisListType

D = 1024
NDC = 8
TCORE = 8192
CHUNK = 1024
TG = 512
NTG = CHUNK // TG
NTPT = TG // 128
E = 20
NSLOT = 4
H4 = 256
EPS = 1e-5


class Ctr:
    def __init__(self, h):
        self.h = h
        self.n = 0

    def inc(self, instr, k=1):
        instr.then_inc(self.h, k)
        self.n += k
        return self.n


class Waiter:
    def __init__(self):
        self.seen = {}

    def wait(self, eng, ctr, val):
        if val <= 0:
            return
        key = id(ctr)
        if self.seen.get(key, -1) >= val:
            return
        eng.wait_ge(ctr.h, val)
        self.seen[key] = val


class Bank:
    def __init__(self, t):
        self.t = t
        self.pe_done = 0
        self.released = None


def build(n_chunk):
    nc = bass.Bass("TRN2", target_bir_lowering=False)
    dt = nc.dram_tensor
    x_fm = dt("x_fm", [D, TCORE], F32, kind="ExternalInput").ap()
    wkt = dt("wkt", [D, D], F32, kind="ExternalInput").ap()
    w1t = dt("w1t", [D, 64], F32, kind="ExternalInput").ap()
    w2t = dt("w2t", [64, D], F32, kind="ExternalInput").ap()
    wg1 = dt("wg1", [D, H4], F32, kind="ExternalInput").ap()
    wg2 = dt("wg2", [H4, 40], F32, kind="ExternalInput").ap()
    bg1m = dt("bg1m", [128, 2], F32, kind="ExternalInput").ap()
    bg2r = dt("bg2r", [1, E], F32, kind="ExternalInput").ap()
    be = dt("be", [E, D], F32, kind="ExternalInput").ap()
    we = dt("we", [E * D, D], F32, kind="ExternalInput").ap()
    jm = dt("jm", [128, 128], F32, kind="ExternalInput").ap()
    onescol = dt("onescol", [128, 1], F32, kind="ExternalInput").ap()
    ones1 = dt("ones1", [1, 128], F32, kind="ExternalInput").ap()
    idn = dt("idn", [128, 128], F32, kind="ExternalInput").ap()
    sel = dt("sel", [E, E * 128], F32, kind="ExternalInput").ap()
    epsc = dt("epsc", [1, 1], F32, kind="ExternalInput").ap()
    out_fm = dt("out_fm", [D, TCORE], F32, kind="ExternalOutput").ap()

    we_r = we.rearrange("(e dc p) o -> e p dc o", e=E, dc=NDC, p=128)
    x_r = x_fm.rearrange("(dc p) t -> p dc t", p=128)
    o_r = out_fm.rearrange("(dc p) t -> p dc t", p=128)

    with ExitStack() as es:
        sb = lambda name, shape, dtype: es.enter_context(nc.sbuf_tensor(name, shape, dtype))
        ps = lambda name, shape, dtype: es.enter_context(nc.psum_tensor(name, shape, dtype))

        X = sb("X", [128, NDC * CHUNK], F32)
        ZN = sb("ZN", [128, NDC * CHUNK], BF16)
        SQ = sb("SQ", [128, 2 * NDC * TG], BF16)
        WS = sb("WS", [128, NSLOT * NDC * D], BF16)
        ZE = sb("ZE", [128, 4 * NDC * TG], BF16)
        ROWA = sb("ROWA", [1, 2 * TG], F32)
        ROWB = sb("ROWB", [1, 2 * TG], F32)
        HB = sb("HB", [128, 2 * TG], BF16)
        ABUF = sb("ABUF", [64, TG], BF16)
        GBB = sb("GBB", [128, 3 * CHUNK], BF16)
        MBS = sb("MBS", [128, 2 * TG], BF16)
        RBS = sb("RBS", [128, 2 * TG], BF16)
        GTB = sb("GTB", [E, CHUNK], BF16)
        GBT = sb("GBT", [128, NTG * NTPT * 32], BF16)
        PRB = sb("PRB", [128, NTPT * E], F32)
        MSK = sb("MSK", [128, NTPT * E], F32)
        TM2 = sb("TM2", [128, NTPT * E], F32)
        NRM = sb("NRM", [128, NTPT], F32)
        SUMS = sb("SUMS", [128, NTPT], F32)
        RS = sb("RS", [128, NTPT], F32)
        M1 = sb("M1", [128, NTPT], F32)
        W1TB = sb("W1TB", [128, NDC * 64], BF16)
        W2TB = sb("W2TB", [64, D], BF16)
        WG1B = sb("WG1B", [128, NDC * H4], BF16)
        WG2B = sb("WG2B", [128, 2 * E], BF16)
        BEB = sb("BEB", [E, D], BF16)
        BG1C = sb("BG1C", [128, 2], F32)
        BG2R = sb("BG2R", [1, E], BF16)
        JB = sb("JB", [128, 128], BF16)
        OCB = sb("OCB", [128, 1], BF16)
        O1B = sb("O1B", [1, 128], BF16)
        IDNB = sb("IDNB", [128, 128], BF16)
        SELB = sb("SELB", [E, E * 128], BF16)
        EPSB = sb("EPSB", [1, 1], F32)

        banks = [Bank(ps(f"pb{i}", [128, 512], F32)) for i in range(7)]
        pGT = ps("pGT", [64, 128], BF16)
        pgt_pe_done = [0]
        pgt_released = [None]

        sT = Ctr(es.enter_context(nc.semaphore("sT")))
        sV = Ctr(es.enter_context(nc.semaphore("sV")))
        sS = Ctr(es.enter_context(nc.semaphore("sS")))
        sG = Ctr(es.enter_context(nc.semaphore("sG")))
        sW = Ctr(es.enter_context(nc.semaphore("sW")))
        sWS = [Ctr(es.enter_context(nc.semaphore(f"sWS{i}"))) for i in range(NSLOT)]
        sXd = [Ctr(es.enter_context(nc.semaphore(f"sXd{i}"))) for i in range(NDC)]
        sOd = [Ctr(es.enter_context(nc.semaphore(f"sOd{i}"))) for i in range(NDC)]

        blk = es.enter_context(nc.Block())
        T, V, S, G, SY = nc.tensor, nc.vector, nc.scalar, nc.gpsimd, nc.sync
        wt, wv, ws, wg, wy = Waiter(), Waiter(), Waiter(), Waiter(), Waiter()

        def pe_write(bank):
            if bank.released is not None:
                wt.wait(T, bank.released[0], bank.released[1])

        def pe_done(bank, instr):
            bank.pe_done = sT.inc(instr)

        def reader_wait(w, eng, bank):
            w.wait(eng, sT, bank.pe_done)

        def release(bank, ctr, instr, pre=None):
            n = pre if pre is not None else ctr.inc(instr)
            bank.released = (ctr, n)
            return n

        G.dma_start(W1TB[:].rearrange("p (kc d) -> p kc d", kc=NDC),
                    w1t.rearrange("(kc p) d -> p kc d", p=128)).then_inc(sW.h, 16)
        G.dma_start(W2TB[:], w2t).then_inc(sW.h, 16)
        G.dma_start(WG1B[:].rearrange("p (dc h) -> p dc h", dc=NDC),
                    wg1.rearrange("(dc p) h -> p dc h", p=128)).then_inc(sW.h, 16)
        G.dma_start(WG2B[:].rearrange("p (hc e) -> p hc e", hc=2),
                    wg2.rearrange("(hc p) e -> p hc e", p=128)[:, :, 0:E]).then_inc(sW.h, 16)
        G.dma_start(BEB[:], be).then_inc(sW.h, 16)
        G.dma_start(BG2R[:], bg2r).then_inc(sW.h, 16)
        G.dma_start(JB[:], jm).then_inc(sW.h, 16)
        G.dma_start(OCB[:], onescol).then_inc(sW.h, 16)
        G.dma_start(O1B[:], ones1).then_inc(sW.h, 16)
        G.dma_start(IDNB[:], idn).then_inc(sW.h, 16)
        G.dma_start(SELB[:], sel).then_inc(sW.h, 16)
        G.dma_start(BG1C[:], bg1m).then_inc(sW.h, 16)
        G.dma_start(EPSB[:], epsc).then_inc(sW.h, 16)
        sW.n += 16 * 13
        init_w = sW.n

        slots_free = [0] * NSLOT
        slot_loaded = [0] * NSLOT
        slot_rr = [0]

        def issue_weight_load(kind, e=None):
            s = slot_rr[0] % NSLOT
            slot_rr[0] += 1
            wg.wait(G, sT, slots_free[s])
            dst = WS[:, s * NDC * D:(s + 1) * NDC * D].rearrange("p (dc o) -> p dc o", dc=NDC)
            if kind == "we":
                G.dma_start(dst, we_r[e]).then_inc(sWS[s].h, 16)
            else:
                G.dma_start(dst, wkt.rearrange("(hc p) k -> p hc k", p=128)).then_inc(sWS[s].h, 16)
            sWS[s].n += 16
            slot_loaded[s] = sWS[s].n
            return s

        ln_pool_prev = [0, 0]
        rbs_guard = [[], []]

        class LNPhase:

            def __init__(self, x_dep):
                self.x_dep = x_dep
                self.tgb = {0: (banks[2], banks[3]), 1: (banks[4], banks[5])}
                self.cast_a = {}
                self.cast_v = {}
                self.subs_done = {}
                self.sq_done = {}
                self.rstd_done = {}
                self.zn_ready = [0, 0]
                self.done_cast = set()
                self.done = set()

            def cast(self, tg, dc):
                if (tg, dc) in self.done_cast:
                    return
                self.done_cast.add((tg, dc))
                t0 = tg * TG
                sl = slice((tg * NDC + dc) * TG, (tg * NDC + dc + 1) * TG)
                xs = X[:, dc * CHUNK + t0:dc * CHUNK + t0 + TG]
                if dc % 2 == 0:
                    for (ctr, cnt) in self.x_dep(tg, dc):
                        ws.wait(S, ctr, cnt)
                    i = S.activation(SQ[:, sl], xs, AF.Copy)
                    self.cast_a[tg] = sS.inc(i)
                else:
                    for (ctr, cnt) in self.x_dep(tg, dc):
                        wv.wait(V, ctr, cnt)
                    i = V.tensor_copy(SQ[:, sl], xs)
                    if dc == NDC - 1:
                        i = V.drain()
                    self.cast_v[tg] = sV.inc(i)

            def casts(self, tg):
                for dc in range(NDC):
                    self.cast(tg, dc)

            def stats_mean(self, tg):
                if ("stats_mean", tg) in self.done:
                    return
                self.done.add(("stats_mean", tg))
                self.casts(tg)
                bM, bR = self.tgb[tg]
                pe_write(bM)
                wt.wait(T, sS, self.cast_a[tg])
                wt.wait(T, sV, self.cast_v[tg])
                for dc in range(NDC):
                    i = T.matmul(bM.t[:], JB[:],
                                 SQ[:, (tg * NDC + dc) * TG:(tg * NDC + dc + 1) * TG],
                                 start=(dc == 0), stop=(dc == NDC - 1))
                pe_done(bM, i)
                reader_wait(ws, S, bM)
                ws.wait(S, sG, ln_pool_prev[tg])
                for (ctr, cnt) in rbs_guard[tg]:
                    ws.wait(S, ctr, cnt)
                i = S.activation(MBS[:, tg * TG:(tg + 1) * TG], bM.t[:], AF.Copy)
                mbs_done = sS.inc(i)
                bM.released = (sS, mbs_done)
                t0 = tg * TG
                rsl = slice(tg * TG, (tg + 1) * TG)
                wv.wait(V, sS, mbs_done)
                zn3 = bass.AP(ZN[:].tensor, t0,
                              [[NDC * CHUNK, 128], [CHUNK, NDC], [1, TG]])
                sq3 = bass.AP(SQ[:].tensor, tg * NDC * TG,
                              [[2 * NDC * TG, 128], [TG, NDC], [1, TG]])
                mb3 = bass.AP(MBS[:].tensor, tg * TG,
                              [[2 * TG, 128], [0, NDC], [1, TG]])
                V.tensor_tensor(zn3, sq3, mb3, ALU.subtract)
                i = V.drain()
                self.subs_done[tg] = sV.inc(i)

            def stats_s2(self, tg):
                if ("stats_s2", tg) in self.done:
                    return
                self.done.add(("stats_s2", tg))
                self.stats_mean(tg)
                bR = self.tgb[tg][1]
                ssl = slice(tg * NDC * TG, (tg + 1) * NDC * TG)
                i = V.tensor_tensor(SQ[:, ssl], SQ[:, ssl], SQ[:, ssl], ALU.mult)
                i = V.drain()
                self.sq_done[tg] = sV.inc(i)
                pe_write(bR)
                wt.wait(T, sV, self.sq_done[tg])
                for dc in range(NDC):
                    i = T.matmul(bR.t[0:1, :], OCB[:],
                                 SQ[:, (tg * NDC + dc) * TG:(tg * NDC + dc + 1) * TG],
                                 start=(dc == 0), stop=(dc == NDC - 1))
                pe_done(bR, i)

            def rows(self, tg):
                if ("rows", tg) in self.done:
                    return
                self.done.add(("rows", tg))
                self.stats_s2(tg)
                bR = self.tgb[tg][1]
                rsl = slice(tg * TG, (tg + 1) * TG)
                reader_wait(ws, S, bR)
                S.drain()
                S.activation(ROWA[0:1, rsl], bR.t[0:1, :], AF.Copy, scale=1.0 / D)
                i = S.activation(ROWB[0:1, rsl], MBS[0:1, rsl], AF.Square)
                rows_done = sS.inc(i)
                wv.wait(V, sS, rows_done)
                V.tensor_tensor(ROWA[0:1, rsl], ROWA[0:1, rsl], ROWB[0:1, rsl], ALU.subtract)
                i = V.drain()
                var_done = sV.inc(i)
                ws.wait(S, sV, var_done)
                S.activation(ROWB[0:1, rsl], ROWA[0:1, rsl], AF.Ln, bias=EPSB[0:1, :])
                S.drain()
                i = S.activation(RBS[0:1, rsl], ROWB[0:1, rsl], AF.Exp, scale=-0.5)
                self.rstd_done[tg] = sS.inc(i)

            def bcast(self, tg):
                if ("bcast", tg) in self.done:
                    return
                self.done.add(("bcast", tg))
                self.rows(tg)
                bR = self.tgb[tg][1]
                rsl = slice(tg * TG, (tg + 1) * TG)
                wt.wait(T, sS, self.rstd_done[tg])
                i = T.matmul(bR.t[:], O1B[:], RBS[0:1, rsl], start=True, stop=True)
                pe_done(bR, i)
                reader_wait(ws, S, bR)
                i = S.activation(RBS[:, rsl], bR.t[:], AF.Copy)
                self.rbs_done = sS.inc(i)
                release(bR, sS, i, pre=self.rbs_done)

            def apply(self, tg):
                if ("apply", tg) in self.done:
                    return
                self.done.add(("apply", tg))
                self.bcast(tg)
                rsl = slice(tg * TG, (tg + 1) * TG)
                t0 = tg * TG
                wv.wait(V, sS, self.rbs_done)
                zn3 = bass.AP(ZN[:].tensor, t0,
                              [[NDC * CHUNK, 128], [CHUNK, NDC], [1, TG]])
                rb3 = bass.AP(RBS[:].tensor, tg * TG,
                              [[2 * TG, 128], [0, NDC], [1, TG]])
                V.tensor_tensor(zn3, zn3, rb3, ALU.mult)
                i = V.drain()
                dve_mul_done = sV.inc(i)
                wg.wait(G, sV, dve_mul_done)
                i = G.engine_nop()
                self.zn_ready[tg] = sG.inc(i)
                ln_pool_prev[tg] = self.zn_ready[tg]
                rbs_guard[tg] = []

            def finish(self):
                for tg in range(NTG):
                    self.apply(tg)
                return self.zn_ready

        def emit_prep(zn_ready, hook=None, stage_hooks=None):
            chains = {}

            def tg_block(tg, shooks):
                t0 = tg * TG
                hb_banks = []
                for hc in range(2):
                    b = banks[hc]
                    pe_write(b)
                    wt.wait(T, sG, zn_ready[tg])
                    for dc in range(NDC):
                        i = T.matmul(b.t[:], WG1B[:, dc * H4 + hc * 128:dc * H4 + hc * 128 + 128],
                                     ZN[:, dc * CHUNK + t0:dc * CHUNK + t0 + TG],
                                     start=(dc == 0), stop=(dc == NDC - 1))
                    pe_done(b, i)
                    hb_banks.append(b)
                for hc in range(2):
                    b = hb_banks[hc]
                    reader_wait(ws, S, b)
                    i = S.activation(HB[:, hc * TG:(hc + 1) * TG], b.t[:], AF.Gelu,
                                     bias=BG1C[:, hc:hc + 1])
                    release(b, sS, i)
                if shooks and "h0" in shooks:
                    shooks["h0"]()
                hb_done = sS.n
                lb = banks[2 + tg]
                pe_write(lb)
                wt.wait(T, sS, hb_done)
                for ts in range(NTPT):
                    T.matmul(lb.t[:, ts * 32:ts * 32 + E],
                             HB[:, 0 * TG + ts * 128:0 * TG + ts * 128 + 128],
                             WG2B[:, 0:E], start=True, stop=False)
                    T.matmul(lb.t[:, ts * 32:ts * 32 + E],
                             HB[:, 1 * TG + ts * 128:1 * TG + ts * 128 + 128],
                             WG2B[:, E:2 * E], start=False, stop=False)
                    i = T.matmul(lb.t[:, ts * 32:ts * 32 + E], O1B[:], BG2R[:],
                                 start=False, stop=True)
                pe_done(lb, i)
                if shooks and "l0" in shooks:
                    shooks["l0"]()
                lb3 = bass.AP(lb.t[:].tensor, 0, [[512, 128], [32, NTPT], [1, E]])
                reader_wait(wv, V, lb)
                V.tensor_reduce(NRM[:], lb3, axis=AX.X, op=ALU.max, negate=True)
                i = V.drain()
                nrm_done = sV.inc(i)
                for ts in range(NTPT):
                    ws.wait(S, sV, nrm_done)
                    i = S.activation(PRB[:, ts * E:(ts + 1) * E], lb.t[:, ts * 32:ts * 32 + E],
                                     AF.Exp, bias=NRM[:, ts:ts + 1], accum_out=SUMS[:, ts:ts + 1])
                ed = sS.inc(i)
                release(lb, sS, i, pre=ed)
                prb3 = PRB[:].rearrange("p (s e) -> p s e", e=E)
                msk3 = MSK[:].rearrange("p (s e) -> p s e", e=E)
                tm23 = TM2[:].rearrange("p (s e) -> p s e", e=E)
                m2b = bass.AP(M1[:].tensor, 0, [[NTPT, 128], [1, NTPT], [0, E]])
                rsb = bass.AP(RS[:].tensor, 0, [[NTPT, 128], [1, NTPT], [0, E]])
                wv.wait(V, sS, ed)
                V.tensor_scalar(MSK[:], PRB[:], 1.0, None, ALU.is_ge)
                V.reciprocal(RS[:], SUMS[:])
                V.drain()
                V.scalar_tensor_tensor(MSK[:], MSK[:], -1e9, PRB[:], ALU.mult, ALU.add)
                V.drain()
                V.tensor_reduce(M1[:], msk3, axis=AX.X, op=ALU.max)
                V.drain()
                V.tensor_tensor(msk3, prb3, m2b, ALU.is_ge)
                V.tensor_tensor(tm23, prb3, rsb, ALU.mult)
                V.drain()
                gbt3 = bass.AP(GBT[:].tensor, tg * NTPT * 32,
                               [[NTG * NTPT * 32, 128], [32, NTPT], [1, E]])
                tm3 = bass.AP(TM2[:].tensor, 0, [[NTPT * E, 128], [E, NTPT], [1, E]])
                mk3 = bass.AP(MSK[:].tensor, 0, [[NTPT * E, 128], [E, NTPT], [1, E]])
                V.tensor_tensor(gbt3, tm3, mk3, ALU.mult)
                i = V.drain()
                chains[tg] = sV.inc(i)

            def transposes(tg):
                for pp in range(NTPT // 2):
                    if pgt_released[0] is not None:
                        wt.wait(T, pgt_released[0][0], pgt_released[0][1])
                    wt.wait(T, sV, chains[tg])
                    i = T.transpose(pGT[:],
                                    GBT[:, (tg * NTPT + 2 * pp) * 32:(tg * NTPT + 2 * pp + 2) * 32],
                                    IDNB[:])
                    pgt_pe_done[0] = sT.inc(i)
                    wv.wait(V, sT, pgt_pe_done[0])
                    ts0 = tg * NTPT + 2 * pp
                    V.tensor_copy(GTB[:, ts0 * 128:(ts0 + 1) * 128], pGT[0:E, :])
                    i = V.tensor_copy(GTB[:, (ts0 + 1) * 128:(ts0 + 2) * 128], pGT[32:32 + E, :])
                    sV.inc(i)
                    pgt_released[0] = (sV, sV.n)
                i = V.drain()
                sV.inc(i)
                if hook is not None:
                    hook(tg)

            tg_block(0, stage_hooks)
            if stage_hooks and "c0" in stage_hooks:
                stage_hooks["c0"]()
            if stage_hooks and "t0" in stage_hooks:
                stage_hooks["t0"]()
            tg_block(1, None)
            transposes(0)
            transposes(1)

        z_free = [0] * 4
        gbb_free = {}

        class ExpertPhase:

            def __init__(self):
                self.gb_banks = [banks[2], banks[3], banks[4], banks[5]]
                self.zorder = []
                for pair in range(E // 2):
                    for tg in range(NTG):
                        for e in (2 * pair, 2 * pair + 1):
                            self.zorder.append((e, tg))
                self.z_done = {}
                self.z_sl = {}
                self.gbb_req = {}
                self.zq = 0
                self.gb_emitted = set()

            def gb_half(self, pair, tg):
                for j in range(2):
                    e = 2 * pair + j
                    b = self.gb_banks[2 * j + tg]
                    pe_write(b)
                    wt.wait(T, sV, self.gtb_done[tg])
                    i = T.matmul(b.t[:], SELB[:, e * 128:(e + 1) * 128],
                                 GTB[:, tg * TG:(tg + 1) * TG], start=True, stop=True)
                    pe_done(b, i)
                for j in range(2):
                    e = 2 * pair + j
                    zs = e % 3
                    b = self.gb_banks[2 * j + tg]
                    ws.wait(S, sV, gbb_free.get((zs, tg), 0))
                    reader_wait(ws, S, b)
                    i = S.activation(GBB[:, zs * CHUNK + tg * TG:zs * CHUNK + (tg + 1) * TG],
                                     b.t[:], AF.Copy)
                    release(b, sS, i)
                    self.gbb_req[(e, tg)] = sS.n
                self.gb_emitted.add((pair, tg))

            def z_pump(self, pe_reachable):
                while self.zq < len(self.zorder):
                    e, tg = self.zorder[self.zq]
                    if (e, tg) not in self.gbb_req:
                        break
                    slot = self.zq % 4
                    if z_free[slot] is None or z_free[slot] > pe_reachable:
                        break
                    prev_free = z_free[slot]
                    self.zq += 1
                    self.z_sl[(e, tg)] = slot
                    z_free[slot] = None
                    wv.wait(V, sS, self.gbb_req[(e, tg)])
                    wv.wait(V, sT, prev_free)
                    wv.wait(V, sG, self.zn_ready[tg])
                    zs = e % 3
                    ze3 = bass.AP(ZE[:].tensor, slot * NDC * TG,
                                  [[4 * NDC * TG, 128], [TG, NDC], [1, TG]])
                    zn3 = bass.AP(ZN[:].tensor, tg * TG,
                                  [[NDC * CHUNK, 128], [CHUNK, NDC], [1, TG]])
                    gb3 = bass.AP(GBB[:].tensor, zs * CHUNK + tg * TG,
                                  [[3 * CHUNK, 128], [0, NDC], [1, TG]])
                    V.tensor_tensor(ze3, zn3, gb3, ALU.mult)
                    i = V.drain()
                    n = sV.inc(i)
                    self.z_done[(e, tg)] = n
                    gbb_free[(zs, tg)] = n

            def prep_hook(self, tg):
                self.gb_half(0, tg)
                self.z_pump(sT.n)

            def run(self, eslots, post_pair=None, tail_hook=None, last_evac=None):
                if last_evac is None:
                    last_evac = {}
                for pair in range(E // 2):
                    e0, e1 = 2 * pair, 2 * pair + 1
                    s0, s1 = eslots[e0], eslots[e1]
                    if pair + 1 < E // 2:
                        for tg in range(NTG):
                            self.gb_half(pair + 1, tg)
                    wt.wait(T, sWS[s0], slot_loaded[s0])
                    wt.wait(T, sWS[s1], slot_loaded[s1])
                    for tg in range(NTG):
                        wt.wait(T, sV, self.z_done[(e0, tg)])
                        wt.wait(T, sV, self.z_done[(e1, tg)])
                        zq0, zq1 = self.z_sl[(e0, tg)], self.z_sl[(e1, tg)]
                        for oc in range(NDC):
                            b = banks[[0, 1, 6][oc % 3]]
                            pe_write(b)
                            first = True
                            if pair == 0:
                                T.matmul(b.t[:], BEB[:, oc * 128:(oc + 1) * 128],
                                         GTB[:, tg * TG:(tg + 1) * TG], start=True, stop=False)
                                first = False
                            for j, (e, s, zz) in enumerate(((e0, s0, zq0), (e1, s1, zq1))):
                                for dc in range(NDC):
                                    i = T.matmul(
                                        b.t[:],
                                        WS[:, s * NDC * D + dc * D + oc * 128:
                                              s * NDC * D + dc * D + oc * 128 + 128],
                                        ZE[:, zz * NDC * TG + dc * TG:zz * NDC * TG + dc * TG + TG],
                                        start=first, stop=(j == 1 and dc == NDC - 1))
                                    first = False
                                    if oc == NDC - 1 and dc == NDC - 1 and j == 0:
                                        z_free[zq0] = sT.inc(i)
                                        if tg == NTG - 1:
                                            slots_free[s0] = z_free[zq0]
                            pe_done(b, i)
                            if oc == NDC - 1:
                                z_free[zq1] = sT.n
                                if tg == NTG - 1:
                                    slots_free[s1] = sT.n
                            stop_n = sT.n
                            wv.wait(V, sT, stop_n)
                            dst = X[:, oc * CHUNK + tg * TG:oc * CHUNK + tg * TG + TG]
                            i = V.scalar_tensor_tensor(dst, b.t[:], 0.5, dst,
                                                       ALU.mult, ALU.add)
                            release(b, sV, i)
                            if pair == E // 2 - 1:
                                i = V.drain()
                                last_evac[(tg, oc)] = sV.inc(i)
                                if tail_hook is not None:
                                    tail_hook(tg, oc)
                            self.z_pump(stop_n)
                    if post_pair is not None:
                        post_pair(pair)
                i = V.drain()
                return sV.inc(i), last_evac

        def emit_attn(wkt_slot, zn_ready, hook=None, att_evac=None, k_hooks=None):
            if att_evac is None:
                att_evac = {}
            k_hooks = k_hooks or {}
            for tg in range(NTG):
                t0 = tg * TG
                kb_dones = []
                for kc in range(NDC):
                    b = banks[[0, 1, 6][kc % 3]]
                    pe_write(b)
                    wt.wait(T, sG, zn_ready[tg])
                    wt.wait(T, sWS[wkt_slot], slot_loaded[wkt_slot])
                    for hc in range(NDC):
                        i = T.matmul(b.t[:],
                                     WS[:, wkt_slot * NDC * D + hc * D + kc * 128:
                                           wkt_slot * NDC * D + hc * D + kc * 128 + 128],
                                     ZN[:, hc * CHUNK + t0:hc * CHUNK + t0 + TG],
                                     start=(hc == 0), stop=(hc == NDC - 1))
                    pe_done(b, i)
                    reader_wait(ws, S, b)
                    i = S.activation(SQ[:, (tg * NDC + kc) * TG:(tg * NDC + kc + 1) * TG],
                                     b.t[:], AF.Copy)
                    release(b, sS, i)
                    kb_dones.append(sS.n)
                    if tg == 0 and kc in k_hooks:
                        k_hooks[kc]()
                b = banks[4]
                pe_write(b)
                for kc in range(NDC):
                    wt.wait(T, sS, kb_dones[kc])
                    i = T.matmul(b.t[0:64, :], W1TB[:, kc * 64:(kc + 1) * 64],
                                 SQ[:, (tg * NDC + kc) * TG:(tg * NDC + kc + 1) * TG],
                                 start=(kc == 0), stop=(kc == NDC - 1))
                pe_done(b, i)
                reader_wait(ws, S, b)
                i = S.activation(ABUF[:], b.t[0:64, :], AF.Gelu)
                ab_done = sS.inc(i)
                release(b, sS, i, pre=ab_done)
                for hc in range(NDC):
                    b = banks[[0, 1, 6][hc % 3]]
                    pe_write(b)
                    wt.wait(T, sS, ab_done)
                    i = T.matmul(b.t[:], W2TB[:, hc * 128:(hc + 1) * 128], ABUF[:],
                                 start=True, stop=True)
                    pe_done(b, i)
                    reader_wait(wv, V, b)
                    i = V.tensor_tensor(X[:, hc * CHUNK + t0:hc * CHUNK + t0 + TG],
                                        X[:, hc * CHUNK + t0:hc * CHUNK + t0 + TG],
                                        b.t[:], ALU.add)
                    release(b, sV, i)
                    i = V.drain()
                    att_evac[(tg, hc)] = sV.inc(i)
                    if hook is not None:
                        hook(tg, hc)
            return sV.n, att_evac

        pending_ln1 = [None]

        def mk_ln_hook(ln, extra=None):
            def hook(tg, idx):
                if extra is not None:
                    extra(tg, idx)
                ln.cast(tg, idx)
                if tg == 1:
                    if idx == 0:
                        ln.stats_mean(0)
                    elif idx == 4:
                        ln.stats_s2(0)
                    elif idx == 5:
                        ln.rows(0)
                    elif idx == 6:
                        ln.bcast(0)
                    elif idx == 7:
                        ln.apply(0)
            return hook

        for c in range(n_chunk):
            if pending_ln1[0] is None:
                x_here = []
                for dc in range(NDC):
                    SY.dma_start(X[:, dc * CHUNK:(dc + 1) * CHUNK],
                                 x_r[:, dc, c * CHUNK:(c + 1) * CHUNK]).then_inc(sXd[dc].h, 16)
                    sXd[dc].n += 16
                    x_here.append(sXd[dc].n)
                xh = list(x_here)
                ln1 = LNPhase(lambda tg, dc: [(sXd[dc], xh[dc])])
            else:
                ln1 = pending_ln1[0]
            if c == 0:
                ws.wait(S, sW, init_w)
                wt.wait(T, sW, init_w)
                wv.wait(V, sW, init_w)
                wg.wait(G, sW, init_w)
            ln1.apply(0)

            my1 = {e: issue_weight_load("we", e) for e in range(4)}
            ep1 = ExpertPhase()
            ep1.zn_ready = ln1.zn_ready
            ep1.gtb_done = [0, 0]
            ln1.stats_mean(1)
            emit_prep(ln1.zn_ready,
                      hook=lambda tg: (ep1.gtb_done.__setitem__(tg, sV.n),
                                       ep1.prep_hook(tg)),
                      stage_hooks={"h0": lambda: ln1.stats_s2(1),
                                   "l0": lambda: ln1.rows(1),
                                   "c0": lambda: ln1.bcast(1),
                                   "t0": lambda: ln1.apply(1)})
            f2pre = {}

            def post_pair1(p):
                for e in (4 + 2 * p, 5 + 2 * p):
                    if e < E:
                        my1[e] = issue_weight_load("we", e)
                if p == 8:
                    f2pre["wkt"] = issue_weight_load("wkt")
                if p == 9:
                    f2pre[0] = issue_weight_load("we", 0)
                    f2pre[1] = issue_weight_load("we", 1)

            f1_evac = {}
            ln2 = LNPhase(lambda tg, dc: [(sV, f1_evac[(tg, dc)])])
            ep1.run(my1, post_pair1, tail_hook=mk_ln_hook(ln2), last_evac=f1_evac)
            ln2.stats_mean(1)
            att_evac = {}
            ln3 = LNPhase(lambda tg, dc: [(sV, att_evac[(tg, dc)])])
            emit_attn(f2pre["wkt"], ln2.zn_ready, hook=mk_ln_hook(ln3), att_evac=att_evac,
                      k_hooks={1: lambda: ln2.stats_s2(1),
                               3: lambda: ln2.rows(1),
                               5: lambda: ln2.bcast(1),
                               6: lambda: ln2.apply(1)})
            slots_free[f2pre["wkt"]] = sT.n
            f2pre[2] = issue_weight_load("we", 2)
            f2pre[3] = issue_weight_load("we", 3)
            ln3.apply(0)
            ln3.stats_mean(1)
            ep2 = ExpertPhase()
            ep2.zn_ready = ln3.zn_ready
            ep2.gtb_done = [0, 0]
            emit_prep(ln3.zn_ready,
                      hook=lambda tg: (ep2.gtb_done.__setitem__(tg, sV.n),
                                       ep2.prep_hook(tg)),
                      stage_hooks={"h0": lambda: ln3.stats_s2(1),
                                   "l0": lambda: ln3.rows(1),
                                   "c0": lambda: ln3.bcast(1),
                                   "t0": lambda: ln3.apply(1)})
            my2 = {e: f2pre[e] for e in range(4)}

            def post_pair2(p):
                for e in (4 + 2 * p, 5 + 2 * p):
                    if e < E:
                        my2[e] = issue_weight_load("we", e)

            f2_evac = {}
            nxt_here = {}

            def store_load(tg, dc):
                hsl = slice(dc * CHUNK + tg * TG, dc * CHUNK + (tg + 1) * TG)
                wy.wait(SY, sV, f2_evac[(tg, dc)])
                wy.wait(SY, sOd[dc], sOd[dc].n)
                wy.wait(SY, sXd[dc], sXd[dc].n)
                SY.dma_start(o_r[:, dc, c * CHUNK + tg * TG:c * CHUNK + (tg + 1) * TG],
                             X[:, hsl]).then_inc(sOd[dc].h, 16)
                sOd[dc].n += 16
                if c + 1 < n_chunk:
                    wy.wait(SY, sOd[dc], sOd[dc].n)
                    SY.dma_start(X[:, hsl],
                                 x_r[:, dc, (c + 1) * CHUNK + tg * TG:
                                            (c + 1) * CHUNK + (tg + 1) * TG]).then_inc(sXd[dc].h, 16)
                    sXd[dc].n += 16
                    nxt_here[(tg, dc)] = sXd[dc].n

            if c + 1 < n_chunk:
                lnext = LNPhase(lambda tg, dc: [(sXd[dc], nxt_here[(tg, dc)])])
                hook2 = mk_ln_hook(lnext, extra=store_load)
                pending_ln1[0] = lnext
            else:
                def hook2(tg, dc):
                    store_load(tg, dc)
                pending_ln1[0] = None
            ep2.run(my2, post_pair2, tail_hook=hook2, last_evac=f2_evac)
        for dc in range(NDC):
            SY.wait_ge(sOd[dc].h, 16 * n_chunk)
    return nc


_NC_CACHE = {}


def kernel(L0_attn_x, L0_attn_W_K, L0_attn_W_V, L0_attn_W1, L0_attn_W2,
           L0_attn_m1, L0_attn_m2, L0_ffn_wg1, L0_ffn_bg1, L0_ffn_wg2,
           L0_ffn_bg2, L0_ffn_We, L0_ffn_be):
    n_cores = 8
    x = np.asarray(L0_attn_x, dtype=np.float32)
    c32 = lambda a: np.ascontiguousarray(np.asarray(a, dtype=np.float32))

    feeds_common = {
        "wkt": c32(np.asarray(L0_attn_W_K).T),
        "w1t": c32(np.asarray(L0_attn_W1).T),
        "w2t": c32(np.asarray(L0_attn_W2).T),
        "wg1": c32(L0_ffn_wg1),
        "wg2": c32(L0_ffn_wg2),
        "bg1m": c32(np.asarray(L0_ffn_bg1).reshape(2, 128).T),
        "bg2r": c32(np.asarray(L0_ffn_bg2)[:E].reshape(1, E)),
        "be": c32(L0_ffn_be),
        "we": c32(np.asarray(L0_ffn_We).reshape(E * D, D)),
        "jm": np.full((128, 128), 1.0 / D, dtype=np.float32),
        "onescol": np.ones((128, 1), dtype=np.float32),
        "ones1": np.ones((1, 128), dtype=np.float32),
        "idn": np.eye(128, dtype=np.float32),
        "sel": np.repeat(np.eye(E, dtype=np.float32), 128, axis=1),
        "epsc": np.full((1, 1), EPS, dtype=np.float32),
    }
    in_maps = []
    for i in range(n_cores):
        f = dict(feeds_common)
        f["x_fm"] = c32(x[i].T)
        in_maps.append(f)

    key = TCORE // CHUNK
    if key not in _NC_CACHE:
        _NC_CACHE[key] = build(key)
    nc = _NC_CACHE[key]
    res = run_bass_kernel_spmd(nc, in_maps, core_ids=list(range(n_cores)))
    out = np.stack([np.asarray(r["out_fm"]).T for r in res.results], axis=0)
    return np.ascontiguousarray(out.astype(np.float32))
